# version 13
# baseline (speedup 1.0000x reference)
import contextlib
import os

import numpy as np
import concourse.bass as bass
import concourse.mybir as mybir
from concourse.bass_utils import run_bass_kernel_spmd

F32 = mybir.dt.float32
BF16 = mybir.dt.bfloat16
F16 = mybir.dt.float16
U8 = mybir.dt.uint8
U16 = mybir.dt.uint16
I32 = mybir.dt.int32
AF = mybir.ActivationFunctionType
ALU = mybir.AluOpType
AXX = mybir.AxisListType.X

S = 2048
Hd = 2048
HD = 2048
TC = 512
NCH = S // TC
ND = HD // 128
NKT = Hd // 128
NN = 16
RR = 128
OG = 2816
NM2 = OG // 128
NSUB = TC // 128

ENGS = ("pe", "dve", "act", "sy", "gp")
XSEMS = ("b0", "b1", "c0", "c1", "wg0", "wg1", "gu", "dn0", "dn1",
         "ow0", "ow1", "xl", "st", "dq")

_TAPS = ()
_LAST = {}


class Prog:
    def __init__(self):
        self.q = {e: [] for e in ENGS}
        self.cnt = {e: 0 for e in ENGS}
        for s_ in XSEMS:
            self.cnt[s_] = 0
        self.pending = {e: [] for e in ENGS}

    def emit(self, eng, fn, waits=(), inc=1, sem=None):
        sem = sem or eng
        w = tuple(waits) + tuple(self.pending[eng])
        w = tuple((s, self.cnt[s]) if s not in ("pe", "dve", "act")
                  and self.cnt[s] > v else (s, v) for (s, v) in w)
        self.pending[eng] = []
        self.q[eng].append((w, fn, inc, sem))
        self.cnt[sem] += inc
        return (sem, self.cnt[sem])

    def mark(self, eng):
        return (eng, self.cnt[eng])

    def barrier(self):
        marks = {e: self.mark(e) for e in (*ENGS, *XSEMS)}
        for e in ENGS:
            self.pending[e].extend(m for k, m in marks.items() if m[1] > 0)


def build_nc(taps=()):
    nc = bass.Bass("TRN2", target_bir_lowering=False)
    taps = set(taps)
    _GPR.clear()

    x_in = nc.dram_tensor("x_in", [S, Hd], F32, kind="ExternalInput")
    w1x = nc.dram_tensor("w1x", [128, NKT * NKT * 128], F32, kind="ExternalInput")
    w1g = nc.dram_tensor("w1g", [128, NKT * NKT * 128], F32, kind="ExternalInput")
    xw = nc.dram_tensor("xw", [HD, 160], F32, kind="ExternalInput")
    dtw = nc.dram_tensor("dtw", [RR, HD], F32, kind="ExternalInput")
    dtbias = nc.dram_tensor("dtbias", [128, ND], F32, kind="ExternalInput")
    alog = nc.dram_tensor("alog", [128, ND * NN], F32, kind="ExternalInput")
    dvec = nc.dram_tensor("dvec", [128, ND], F32, kind="ExternalInput")
    convw = nc.dram_tensor("convw", [128, ND * 4], F32, kind="ExternalInput")
    convb = nc.dram_tensor("convb", [128, ND], F32, kind="ExternalInput")
    wo = nc.dram_tensor("wo", [128, ND * NKT * 128], F32, kind="ExternalInput")
    gpk = nc.dram_tensor("gpk", [Hd // 2, OG], U8, kind="ExternalInput")
    upk = nc.dram_tensor("upk", [Hd // 2, OG], U8, kind="ExternalInput")
    dpk = nc.dram_tensor("dpk", [OG // 2, Hd], U8, kind="ExternalInput")
    gsc = nc.dram_tensor("gsc", [16, OG], F16, kind="ExternalInput")
    usc = nc.dram_tensor("usc", [16, OG], F16, kind="ExternalInput")
    dsc = nc.dram_tensor("dsc", [22, Hd], F16, kind="ExternalInput")
    fvals = nc.dram_tensor("fvals", [1, 8], I32, kind="ExternalInput")
    out = nc.dram_tensor("out", [Hd, S // 2], F32, kind="ExternalOutput")

    ident_d = nc.inline_tensor(np.eye(128, dtype=np.float32), "identc")

    wg_f = nc.dram_tensor("wg_f", [128, NM2 * NKT * 128], F32)
    wu_f = nc.dram_tensor("wu_f", [128, NM2 * NKT * 128], F32)
    wd_f = nc.dram_tensor("wd_f", [128, NKT * NM2 * 128], F32)
    bscr = nc.dram_tensor("bscr", [NN, S], F32)
    cscr = nc.dram_tensor("cscr", [NN, S], F32)
    ffsum = nc.dram_tensor("ffsum", [Hd, 2 * TC], F32)
    ygs = nc.dram_tensor("ygs", [128, ND * TC], F32)
    sh_at = nc.dram_tensor("sh_at", [2, Hd, S], F32, addr_space="Shared")
    sh_ff = nc.dram_tensor("sh_ff", [2, Hd, S], F32, addr_space="Shared")
    sh_fl = nc.dram_tensor("shflag", [2, 16], I32, addr_space="Shared")

    tap_shapes = {
        "xn": [TC, Hd], "xnT": [128, NKT * TC], "xs": [128, ND * TC],
        "dtin": [RR, TC], "bmat": [16, TC], "cmat": [16, TC],
        "yg": [128, ND * TC], "hffn": [128, NM2 * TC],
        "atp": [128, NKT * TC], "ffp": [128, NKT * TC],
        "wg0": [128, OG], "wu0": [128, OG], "wd0": [128, Hd], "xf": [128, NKT * TC],
        "wgF": [Hd, OG], "wuF": [Hd, OG],
        "shat": [Hd, S], "shff": [Hd, S],
    }
    tap_t = {t: nc.dram_tensor("tap_" + t, tap_shapes[t], F32, kind="ExternalOutput")
             for t in taps}

    P = Prog()

    with contextlib.ExitStack() as _es:
        c_ident = _es.enter_context(nc.sbuf_tensor("c_ident", [128, 128], F32))
        c_dtb = _es.enter_context(nc.sbuf_tensor("c_dtb", [128, ND], F32))
        c_A = _es.enter_context(nc.sbuf_tensor("c_A", [128, ND * NN], F32))
        c_dv = _es.enter_context(nc.sbuf_tensor("c_dv", [128, ND], F32))
        c_cw = _es.enter_context(nc.sbuf_tensor("c_cw", [128, ND * 4], F32))
        c_cb = _es.enter_context(nc.sbuf_tensor("c_cb", [128, ND], F32))
        c_dtw = _es.enter_context(nc.sbuf_tensor("c_dtw", [128, HD], F32))
        c_ones = _es.enter_context(nc.sbuf_tensor("c_ones", [128, 1], F32))
        c_eps = _es.enter_context(nc.sbuf_tensor("c_eps", [128, 2], F32))
        c_fv = _es.enter_context(nc.sbuf_tensor("c_fv", [1, 8], I32))
        carry = _es.enter_context(nc.sbuf_tensor("carry", [128, ND * NN], F32))
        halo = _es.enter_context(nc.sbuf_tensor("halo", [128, ND * 4], F32))
        fl_t = _es.enter_context(nc.sbuf_tensor("fl_t", [1, 1], I32))
        xn1 = _es.enter_context(nc.sbuf_tensor("xn1", [128, Hd], F32))
        ln_s = _es.enter_context(nc.sbuf_tensor("ln_s", [128, 4], F32))
        xnT = _es.enter_context(nc.sbuf_tensor("xnT", [128, NKT * TC], F32))
        xs_t = _es.enter_context(nc.sbuf_tensor("xs_t", [128, ND * TC], F32))
        hff = _es.enter_context(nc.sbuf_tensor("hff", [128, NM2 * TC], F32))
        w1b = _es.enter_context(nc.sbuf_tensor("w1b", [128, 2 * NKT * 128], F32))
        war = _es.enter_context(nc.sbuf_tensor("war", [128, 4 * NKT * 128], F32))
        ps0 = _es.enter_context(nc.psum_tensor("ps0", [128, TC], F32))
        ps1 = _es.enter_context(nc.psum_tensor("ps1", [128, TC], F32))
        ps2 = _es.enter_context(nc.psum_tensor("ps2", [128, TC], F32))
        ps3 = _es.enter_context(nc.psum_tensor("ps3", [128, TC], F32))
        ps4 = _es.enter_context(nc.psum_tensor("ps4", [128, TC], F32))
        ps5 = _es.enter_context(nc.psum_tensor("ps5", [128, TC], F32))
        ps6 = _es.enter_context(nc.psum_tensor("ps6", [128, TC], F32))
        ps7 = _es.enter_context(nc.psum_tensor("ps7", [128, TC], F32))
        m_pe = _es.enter_context(nc.semaphore("m_pe"))
        m_dve = _es.enter_context(nc.semaphore("m_dve"))
        m_act = _es.enter_context(nc.semaphore("m_act"))
        m_sy = _es.enter_context(nc.semaphore("m_sy"))
        m_gp = _es.enter_context(nc.semaphore("m_gp"))
        m_poll = _es.enter_context(nc.semaphore("m_poll"))
        m_xs = {s_: _es.enter_context(nc.semaphore("m_" + s_)) for s_ in XSEMS}
        block = _es.enter_context(nc.Block())
        SEM = {"pe": m_pe, "dve": m_dve, "act": m_act, "sy": m_sy, "gp": m_gp}
        SEM.update(m_xs)

        c_A2 = xnT[:, 0:ND * NN]

        def ktoff(kt):
            return 256 * (kt >> 1) + (kt & 1)

        def ld(eng, dst, src):
            return P.emit(eng, lambda e, d=dst, s=src: e.dma_start(out=d, in_=s),
                          inc=16)

        ld("gp", c_ident[:], ident_d.ap())
        ld("gp", c_dtb[:], dtbias.ap())
        ld("gp", c_A2[:], alog.ap())
        ld("gp", c_dv[:], dvec.ap())
        ld("gp", c_cw[:], convw.ap())
        ld("gp", c_cb[:], convb.ap())
        ld("gp", c_dtw[:], dtw.ap())
        ld("gp", c_fv[:], fvals.ap())
        k0 = P.mark("gp")
        P.emit("act", lambda e: e.activation(out=c_A[:], in_=c_A2[:], func=AF.Exp),
               waits=[k0])
        aexp = P.mark("act")
        P.emit("dve", lambda e: e.tensor_scalar(c_A[:], c_A[:], -1.0, None, ALU.mult),
               waits=[aexp])
        P.emit("dve", lambda e: e.memset(carry[:], 0.0))
        P.emit("dve", lambda e: e.memset(halo[:], 0.0))
        P.emit("dve", lambda e: e.memset(c_ones[:], 1.0))
        P.emit("dve", lambda e: e.memset(c_eps[:, 0:1], 1e-5))
        P.emit("dve", lambda e: e.memset(c_eps[:, 1:2], 1e-6))

        OH = 768

        _dq = contextlib.ExitStack()
        q_b8 = _dq.enter_context(nc.sbuf_tensor("q_b8", [128, OH], U8))
        q_w16 = _dq.enter_context(nc.sbuf_tensor("q_w16", [128, OH], U16))
        q_nib = _dq.enter_context(nc.sbuf_tensor("q_nib", [128, OH], U16))
        q_t1 = _dq.enter_context(nc.sbuf_tensor("q_t1", [128, OH], U16))
        q_t2 = _dq.enter_context(nc.sbuf_tensor("q_t2", [128, OH], U16))
        q_v1 = _dq.enter_context(nc.sbuf_tensor("q_v1", [128, OH], BF16))
        q_s16 = _dq.enter_context(nc.sbuf_tensor("q_s16", [128, OH], F16))
        q_s32 = _dq.enter_context(nc.sbuf_tensor("q_s32", [128, OH], F32))
        q_p = _dq.enter_context(nc.sbuf_tensor("q_p", [128, 2 * OH], F32))
        if True:
            s32_prev = [None]

            def dequant(src_pk, src_sc, n_blk, ocols, dst, perm_cols, mstride):
                if perm_cols:
                    ohs = [(0, 768), (768, 768), (1536, 768), (2304, 512)]
                else:
                    ohs = [(0, 768), (768, 768), (1536, 512)]
                for b_ in range(n_blk):
                    for (obase, oc) in ohs:
                        l0 = ld("sy", q_b8[:, :oc],
                                src_pk.ap()[b_ * 128:(b_ + 1) * 128, obase:obase + oc])
                        src_s = bass.AP(src_sc, 2 * b_ * ocols + obase,
                                        [[ocols, 2], [0, 64], [1, oc]])
                        l1 = P.emit("sy", lambda e, s=src_s, c_=oc: e.dma_start(
                            out=q_s16[:, :c_], in_=s),
                            waits=([s32_prev[0]] if s32_prev[0] else []), inc=16)
                        P.emit("dve", lambda e, c_=oc: e.tensor_copy(
                            q_w16[:, :c_], q_b8[:, :c_]), waits=[l0])
                        P.emit("dve", lambda e, c_=oc: e.tensor_copy(
                            q_s32[:, :c_], q_s16[:, :c_]), waits=[l1])
                        s32_prev[0] = P.mark("dve")
                        for par in range(2):
                            if par == 0:
                                P.emit("dve", lambda e, c_=oc: e.tensor_scalar(
                                    q_nib[:, :c_], q_w16[:, :c_], 15, None,
                                    ALU.bitwise_and))
                            else:
                                P.emit("dve", lambda e, c_=oc: e.tensor_scalar(
                                    q_nib[:, :c_], q_w16[:, :c_], 4, 15,
                                    ALU.logical_shift_right, ALU.bitwise_and))
                            P.emit("dve", lambda e, c_=oc: e.tensor_scalar(
                                q_t1[:, :c_], q_nib[:, :c_], 7, 6,
                                ALU.bitwise_and, ALU.logical_shift_left))
                            P.emit("dve", lambda e, c_=oc: e.tensor_scalar(
                                q_t2[:, :c_], q_nib[:, :c_], 8, 12,
                                ALU.bitwise_and, ALU.logical_shift_left))
                            P.emit("dve", lambda e, c_=oc: e.tensor_tensor(
                                q_t1[:, :c_], q_t1[:, :c_], q_t2[:, :c_],
                                ALU.bitwise_or))
                            P.emit("dve", lambda e, c_=oc: e.tensor_scalar(
                                q_v1[:, :c_], q_t1[:, :c_].bitcast(BF16),
                                float(2.0 ** 63), float(2.0 ** 63),
                                ALU.mult, ALU.mult))
                            if perm_cols:
                                nb = oc // 256
                                po = bass.AP(q_p, par * OH,
                                             [[2 * OH, 128], [256, nb], [1, 128], [128, 2]])
                                pv = bass.AP(q_v1, 0,
                                             [[OH, 128], [256, nb], [2, 128], [1, 2]])
                                psc = bass.AP(q_s32, 0,
                                              [[OH, 128], [256, nb], [2, 128], [1, 2]])
                                P.emit("dve", lambda e, a=po, v=pv, s=psc:
                                       e.tensor_tensor(a, v, s, ALU.mult))
                            else:
                                P.emit("dve", lambda e, p_=par, c_=oc: e.tensor_tensor(
                                    q_p[:, p_ * OH:p_ * OH + c_], q_v1[:, :c_],
                                    q_s32[:, :c_], ALU.mult))
                        dv = P.mark("dve")
                        rl = n_blk * 2 * 128 * mstride // 128 if False else None
                        for par in range(2):
                            kt2 = None
                            P.emit("sy", lambda e, b2=b_, p2=par, ob=obase, c_=oc,
                                   d2=dst, ms=mstride: e.dma_start(
                                       out=bass.AP(
                                           d2, (ob // 128) * ms + (2 * b2 + p2) * 128,
                                           [[NM2 * NKT * 128, 128],
                                            [ms, c_ // 128], [1, 128]]),
                                       in_=q_p[:, p2 * OH:p2 * OH + c_]),
                                   waits=[dv], inc=16, sem="dq")

            dequant(gpk, gsc, 8, OG, wg_f, True, NKT * 128)
            dequant(upk, usc, 8, OG, wu_f, True, NKT * 128)
            dequant(dpk, dsc, 11, Hd, wd_f, False, NM2 * 128)
        _dq.close()
        deq_gp = P.mark("dq")
        deq_dve = P.mark("dve")

        for t, d2 in (("wg0", wg_f), ("wu0", wu_f), ("wd0", wd_f)):
            if t in taps:
                P.emit("gp", lambda e, t_=t, s=d2: e.dma_start(
                    out=tap_t[t_].ap(), in_=s.ap()[0:128, :]),
                    waits=[deq_gp], inc=16)
        for t, d2 in (("wgF", wg_f), ("wuF", wu_f)):
            if t in taps:
                P.emit("gp", lambda e, t_=t, s=d2: e.dma_start(
                    out=tap_t[t_].ap(), in_=s.ap()),
                    waits=[deq_gp], inc=16)
        if taps:
            P.barrier()

        xpj = _es.enter_context(nc.sbuf_tensor("xpj", [128, TC + 3], F32))
        dtin = _es.enter_context(nc.sbuf_tensor("dtin", [128, TC], F32))
        dtsp = _es.enter_context(nc.sbuf_tensor("dtsp", [128, 2 * TC], F32))
        dtx = _es.enter_context(nc.sbuf_tensor("dtx", [128, 2 * TC], F32))
        brep = _es.enter_context(nc.sbuf_tensor("brep", [128, TC], F32))
        crep = _es.enter_context(nc.sbuf_tensor("crep", [128, TC], F32))
        dAt = _es.enter_context(nc.sbuf_tensor("dAt", [128, TC], F32))
        dBu = _es.enter_context(nc.sbuf_tensor("dBu", [128, TC], F32))
        stt_ = _es.enter_context(nc.sbuf_tensor("stt_", [128, TC], F32))
        tmp_ = _es.enter_context(nc.sbuf_tensor("tmp_", [128, TC], F32))
        ydt = _es.enter_context(nc.sbuf_tensor("ydt", [128, TC], F32))
        sg_t = _es.enter_context(nc.sbuf_tensor("sg_t", [128, TC], F32))
        ev_a = _es.enter_context(nc.sbuf_tensor("ev_a", [128, 2 * TC], F32))
        if True:
            c_xw = war[:, 5632:5632 + ND * 160]
            bc_s = sg_t
            wgb = war[:, 0:NKT * 128]
            wub = war[:, NKT * 128:2 * NKT * 128]
            wdb = war[:, 0:2 * NM2 * 128]
            wob = war[:, 0:2 * NKT * 128]
            z_sp = tmp_
            for c in range(NCH):
                t0 = c * TC
                P.barrier()

                ev_done = []
                for sub in range(NSUB):
                    lx = P.emit("gp", lambda e, s=sub, t_=t0: e.dma_start(
                        out=xn1[:], in_=x_in.ap()[t_ + s * 128:t_ + (s + 1) * 128, :]),
                        waits=([ev_done[-1]] if ev_done else []), inc=16, sem="xl")
                    P.emit("dve", lambda e: e.reduce_sum(ln_s[:, 0:1], xn1[:], axis=AXX),
                           waits=[lx])
                    rdc_m = P.mark("dve")
                    P.emit("dve", lambda e: e.tensor_scalar(
                        ln_s[:, 1:2], ln_s[:, 0:1], 1.0 / Hd, None, ALU.mult),
                        waits=[rdc_m])
                    mu_m = P.mark("dve")
                    P.emit("dve", lambda e: e.tensor_scalar(
                        xn1[:], xn1[:], ln_s[:, 1:2], None, ALU.subtract),
                        waits=[mu_m])
                    xc_m = P.mark("dve")
                    P.emit("act", lambda e: e.activation(
                        out=xs_t[:, 0:Hd], in_=xn1[:],
                        func=AF.Square, accum_out=ln_s[:, 2:3]), waits=[xc_m])
                    sq_m = P.mark("act")
                    P.emit("act", lambda e: e.activation(
                        out=ln_s[:, 3:4], in_=ln_s[:, 2:3], func=AF.Sqrt,
                        scale=1.0 / Hd, bias=c_eps[:, 0:1]), waits=[sq_m])
                    sd_m = P.mark("act")
                    P.emit("dve", lambda e: e.reciprocal(ln_s[:, 2:3], ln_s[:, 3:4]),
                           waits=[sd_m])
                    rs_self = P.mark("dve")
                    P.emit("dve", lambda e: e.tensor_scalar(
                        xn1[:], xn1[:], ln_s[:, 2:3], None, ALU.mult),
                        waits=[rs_self])
                    xn_m = P.mark("dve")
                    if "xn" in taps and c == 0:
                        P.emit("gp", lambda e, s=sub: e.dma_start(
                            out=tap_t["xn"].ap()[s * 128:(s + 1) * 128, :], in_=xn1[:]),
                            waits=[xn_m], inc=16)
                        P.barrier()
                    for kt in range(NKT):
                        pst = ps0 if (sub * NKT + kt) % 2 == 0 else ps1
                        src = bass.AP(xn1, ktoff(kt), [[Hd, 128], [2, 128]])
                        w = [xn_m]
                        if len(ev_done) >= 2:
                            w.append(ev_done[-2])
                        P.emit("pe", lambda e, p_=pst, s_=src: e.transpose(
                            p_[:, 0:128], s_, c_ident[:]), waits=w)
                        tp_m = P.mark("pe")
                        P.emit("act", lambda e, p_=pst, k_=kt, s_=sub: e.activation(
                            out=xnT[:, k_ * TC + s_ * 128:k_ * TC + s_ * 128 + 128],
                            in_=p_[:, 0:128], func=AF.Identity), waits=[tp_m])
                        ev_done.append(P.mark("act"))
                xnT_done = ev_done[-1]
                if "xnT" in taps and c == 0:
                    P.emit("gp", lambda e: e.dma_start(out=tap_t["xnT"].ap(), in_=xnT[:]),
                           waits=[xnT_done], inc=16)
                    P.barrier()

                for k in range(ND):
                    P.emit("sy", lambda e, k_=k: e.dma_start(
                        out=c_xw[:, k_ * 160:(k_ + 1) * 160],
                        in_=xw.ap()[k_ * 128:(k_ + 1) * 128, :]), inc=16, sem="gu")
                cxw_m = P.mark("gu")
                conv_marks = []
                for k in range(ND):
                    buf = k % 2
                    wsl = slice(buf * NKT * 128, (buf + 1) * NKT * 128)
                    lw = P.emit("gp", lambda e, k_=k, ws=wsl: e.dma_start(
                        out=w1b[:, ws],
                        in_=bass.AP(w1x, k_ * NKT * 128,
                                    [[NKT * NKT * 128, 128], [1, NKT * 128]])),
                        waits=([conv_marks[k - 2][2]] if k >= 2 else []), inc=16,
                        sem="wg0" if k % 2 == 0 else "wg1")
                    psD = ps2 if k % 2 == 0 else ps3
                    w_mm = [lw, xnT_done]
                    if k >= 2:
                        w_mm.append(conv_marks[k - 2][1])
                    for kt in range(NKT):
                        P.emit("pe", lambda e, k2=kt, ws=wsl, p_=psD: e.matmul(
                            p_[:], w1b[:, ws.start + k2 * 128:ws.start + (k2 + 1) * 128],
                            xnT[:, k2 * TC:(k2 + 1) * TC],
                            start=(k2 == 0), stop=(k2 == NKT - 1)),
                            waits=(w_mm if kt == 0 else []))
                    mm_m = P.mark("pe")
                    xo = 0
                    wev = [mm_m]
                    if c == 0 and k == 0:
                        wev += [deq_dve, deq_gp]
                    P.emit("dve", lambda e, p_=psD, o=xo: e.tensor_copy(
                        xpj[:, o + 3:o + 3 + TC], p_[:]), waits=wev)
                    P.emit("dve", lambda e, k_=k, o=xo: e.tensor_copy(
                        xpj[:, o:o + 3], halo[:, k_ * 4:k_ * 4 + 3]))
                    P.emit("dve", lambda e, k_=k, o=xo: e.tensor_copy(
                        halo[:, k_ * 4:k_ * 4 + 3], xpj[:, o + TC:o + TC + 3]))
                    hlo_m = P.mark("dve")
                    P.emit("dve", lambda e, k_=k, o=xo: e.tensor_scalar(
                        tmp_[:, 0:TC], xpj[:, o:o + TC],
                        c_cw[:, k_ * 4:k_ * 4 + 1], None, ALU.mult),
                        waits=[hlo_m])
                    for kk in range(1, 4):
                        P.emit("dve", lambda e, k_=k, o=xo, q=kk: e.scalar_tensor_tensor(
                            tmp_[:, 0:TC], xpj[:, o + q:o + q + TC],
                            c_cw[:, k_ * 4 + q:k_ * 4 + q + 1], tmp_[:, 0:TC],
                            ALU.mult, ALU.add))
                    cm = P.mark("dve")
                    P.emit("act", lambda e, k_=k: e.activation(
                        out=xs_t[:, k_ * TC:(k_ + 1) * TC], in_=tmp_[:, 0:TC],
                        func=AF.Silu, bias=c_cb[:, k_:k_ + 1]), waits=[cm])
                    xs_m = P.mark("act")
                    conv_marks.append((k, xs_m, P.mark("pe")))
                for k in range(ND):
                    w_xp = [conv_marks[k][1]] + ([cxw_m] if k == 0 else [])
                    P.emit("pe", lambda e, k_=k: e.matmul(
                        ps0[:, 0:TC], c_xw[:, k_ * 160:k_ * 160 + 128],
                        xs_t[:, k_ * TC:(k_ + 1) * TC],
                        start=(k_ == 0), stop=(k_ == ND - 1)), waits=w_xp)
                    P.emit("pe", lambda e, k_=k: e.matmul(
                        ps1[0:32, 0:TC], c_xw[:, k_ * 160 + 128:(k_ + 1) * 160],
                        xs_t[:, k_ * TC:(k_ + 1) * TC],
                        start=(k_ == 0), stop=(k_ == ND - 1)))
                xp_done = P.mark("pe")
                if "xs" in taps and c == 0:
                    P.emit("gp", lambda e: e.dma_start(out=tap_t["xs"].ap(), in_=xs_t[:]),
                           waits=[conv_marks[-1][1]], inc=16)
                    P.barrier()

                P.emit("dve", lambda e: e.tensor_copy(dtin[:], ps0[:, 0:TC]),
                       waits=[xp_done])
                P.emit("dve", lambda e: e.tensor_copy(bc_s[0:32, 0:TC], ps1[0:32, 0:TC]))
                bc_m = P.mark("dve")
                P.emit("gp", lambda e, t_=t0: e.dma_start(
                    out=bscr.ap()[:, t_:t_ + TC], in_=bc_s[0:16, 0:TC]),
                    waits=[bc_m], inc=16)
                P.emit("gp", lambda e, t_=t0: e.dma_start(
                    out=cscr.ap()[:, t_:t_ + TC], in_=bc_s[16:32, 0:TC]),
                    waits=[bc_m], inc=16)
                bc_hbm = P.mark("gp")
                for t_, sl_ in (("dtin", None), ("bmat", (0, 16)), ("cmat", (16, 32))):
                    if t_ in taps and c == 0:
                        srcap = dtin[:] if sl_ is None else bc_s[sl_[0]:sl_[1], 0:TC]
                        P.emit("gp", lambda e, a=tap_t[t_], s=srcap: e.dma_start(
                            out=a.ap(), in_=s), waits=[bc_m], inc=16)
                        P.barrier()

                dti_m = P.mark("dve")
                h_marks = []
                dn_ev = []
                gu_mm = {}
                dn_mm = {}

                def feed_gu_mm(m):
                    wwait = [gu_mm[m - 1][0]] if m >= 1 else [dti_m]
                    lg = P.emit("sy", lambda e, m_=m: e.dma_start(
                        out=wgb[:],
                        in_=bass.AP(wg_f, m_ * NKT * 128,
                                    [[NM2 * NKT * 128, 128], [1, NKT * 128]])),
                        waits=wwait + [deq_gp], inc=16, sem="gu")
                    lu = P.emit("sy", lambda e, m_=m: e.dma_start(
                        out=wub[:],
                        in_=bass.AP(wu_f, m_ * NKT * 128,
                                    [[NM2 * NKT * 128, 128], [1, NKT * 128]])),
                        inc=16, sem="gu")
                    psA, psB = (ps2, ps3) if m % 2 == 0 else (ps4, ps5)
                    w_mm = [lg, lu, xnT_done]
                    w_mm.append(h_marks[m - 2][1] if m >= 2 else dti_m)
                    for kt in range(NKT):
                        P.emit("pe", lambda e, k_=kt, pA=psA: e.matmul(
                            pA[:], wgb[:, k_ * 128:(k_ + 1) * 128],
                            xnT[:, k_ * TC:(k_ + 1) * TC],
                            start=(k_ == 0), stop=(k_ == NKT - 1)),
                            waits=(w_mm if kt == 0 else []))
                    for kt in range(NKT):
                        P.emit("pe", lambda e, k_=kt, pB=psB: e.matmul(
                            pB[:], wub[:, k_ * 128:(k_ + 1) * 128],
                            xnT[:, k_ * TC:(k_ + 1) * TC],
                            start=(k_ == 0), stop=(k_ == NKT - 1)))
                    gu_mm[m] = (P.mark("pe"), psA, psB)

                def feed_gu_post(m):
                    mm_m, psA, psB = gu_mm[m]
                    sw = [mm_m] + ([h_marks[m - 1][1]] if m >= 1 else [])
                    P.emit("act", lambda e, pA=psA: e.activation(
                        out=xpj[:, 0:TC], in_=pA[:], func=AF.Silu), waits=sw)
                    sg_m = P.mark("act")
                    P.emit("dve", lambda e, m_=m, pB=psB: e.tensor_tensor(
                        hff[:, m_ * TC:(m_ + 1) * TC], xpj[:, 0:TC],
                        pB[:], ALU.mult), waits=[sg_m])
                    h_marks.append((m, P.mark("dve")))

                DNB = [ps7, ps2, ps3, ps4, ps5]

                def feed_dn_mm(m):
                    buf = m % 2
                    wsl = slice(buf * NM2 * 128, (buf + 1) * NM2 * 128)
                    lw = P.emit("sy", lambda e, m_=m, ws=wsl: e.dma_start(
                        out=wdb[:, ws],
                        in_=bass.AP(wd_f, m_ * NM2 * 128,
                                    [[NKT * NM2 * 128, 128], [1, NM2 * 128]])),
                        waits=([dn_mm[m - 2][0]] if m >= 2 else [h_marks[-1][1]]),
                        inc=16, sem="dn0" if m % 2 == 0 else "dn1")
                    psC = DNB[m % 5]
                    w_mm = [lw, h_marks[-1][1]]
                    if m >= 5:
                        w_mm.append(dn_ev[m - 5][1])
                    for kt in range(NM2):
                        P.emit("pe", lambda e, k_=kt, ws=wsl, p_=psC: e.matmul(
                            p_[:], wdb[:, ws.start + k_ * 128:ws.start + (k_ + 1) * 128],
                            hff[:, k_ * TC:(k_ + 1) * TC],
                            start=(k_ == 0), stop=(k_ == NM2 - 1)),
                            waits=(w_mm if kt == 0 else []))
                    dn_mm[m] = (P.mark("pe"), psC)

                def feed_dn_post(m):
                    mm_m, psC = dn_mm[m]
                    eb = m % 2
                    evw = [mm_m] + ([dn_ev[m - 2][2]] if m >= 2 else [])
                    P.emit("dve", lambda e, p_=psC, b_=eb: e.tensor_copy(
                        ev_a[:, b_ * TC:(b_ + 1) * TC], p_[:]), waits=evw)
                    evm = P.mark("dve")
                    gpw = P.emit("sy", lambda e, m_=m, b_=eb, t_=t0: _br2(
                        e,
                        lambda e2: e2.dma_start(
                            out=bass.AP(sh_ff, 0 * Hd * S + m_ * 128 * S + t_,
                                        [[S, 128], [1, TC]]),
                            in_=ev_a[:, b_ * TC:(b_ + 1) * TC]),
                        lambda e2: e2.dma_start(
                            out=bass.AP(sh_ff, 1 * Hd * S + m_ * 128 * S + t_,
                                        [[S, 128], [1, TC]]),
                            in_=ev_a[:, b_ * TC:(b_ + 1) * TC]),
                        m_xs["st"]), waits=[evm], inc=16, sem="st")
                    dn_ev.append((m, evm, gpw))
                    if "ffp" in taps and c == 0:
                        P.emit("gp", lambda e, m_=m, b_=eb: e.dma_start(
                            out=tap_t["ffp"].ap()[:, m_ * TC:(m_ + 1) * TC],
                            in_=ev_a[:, b_ * TC:(b_ + 1) * TC]), waits=[evm], inc=16)
                        P.barrier()

                yg_marks = []
                yst_marks = []
                zsp_marks = []
                dsp_marks = []
                sc_dbu, sc_cm, sc_scan = [], [], []
                for k in range(ND):
                    if 1 <= k <= 11:
                        feed_gu_post(2 * (k - 1))
                        feed_gu_post(2 * (k - 1) + 1)
                    db = k % 2
                    w_dt = [dti_m]
                    if k >= 1:
                        w_dt.append(zsp_marks[k - 1])
                    P.emit("pe", lambda e, k_=k: e.matmul(
                        ps0[:, 0:TC], c_dtw[:, k_ * 128:(k_ + 1) * 128], dtin[:],
                        start=True, stop=True), waits=w_dt)
                    dtp_m = P.mark("pe")
                    zw = [dtp_m]
                    if k >= 1:
                        zw.append(dsp_marks[k - 1])
                    P.emit("act", lambda e, k_=k: e.activation(
                        out=z_sp[:, 0:TC], in_=ps0[:, 0:TC], func=AF.Identity,
                        bias=c_dtb[:, k_:k_ + 1]), waits=zw)
                    zsp_marks.append(P.mark("act"))
                    aw = [dsp_marks[k - 2]] if k >= 2 else []
                    P.emit("act", lambda e, b_=db: e.activation(
                        out=dtsp[:, b_ * TC:(b_ + 1) * TC], in_=z_sp[:, 0:TC],
                        func=AF.Abs), waits=aw)
                    P.emit("act", lambda e, b_=db: e.activation(
                        out=dtsp[:, b_ * TC:(b_ + 1) * TC],
                        in_=dtsp[:, b_ * TC:(b_ + 1) * TC], func=AF.Exp, scale=-1.0))
                    P.emit("act", lambda e, b_=db: e.activation(
                        out=dtsp[:, b_ * TC:(b_ + 1) * TC],
                        in_=dtsp[:, b_ * TC:(b_ + 1) * TC], func=AF.Ln, bias=1.0))
                    spl_m = P.mark("act")
                    P.emit("dve", lambda e, b_=db: e.scalar_tensor_tensor(
                        dtsp[:, b_ * TC:(b_ + 1) * TC], z_sp[:, 0:TC], 0.0,
                        dtsp[:, b_ * TC:(b_ + 1) * TC], ALU.max, ALU.add),
                        waits=[spl_m])
                    P.emit("dve", lambda e, k_=k, b_=db: e.tensor_tensor(
                        dtx[:, b_ * TC:(b_ + 1) * TC], dtsp[:, b_ * TC:(b_ + 1) * TC],
                        xs_t[:, k_ * TC:(k_ + 1) * TC], ALU.mult))
                    dsp_m = P.mark("dve")
                    dsp_marks.append(dsp_m)
                    gbuf = k % 2
                    gsl = slice(gbuf * NKT * 128, (gbuf + 1) * NKT * 128)
                    gw = [xp_done] if k < 2 else [yg_marks[k - 2][3]]
                    lwg = P.emit("sy", lambda e, k_=k, ws=gsl: e.dma_start(
                        out=w1b[:, ws],
                        in_=bass.AP(w1g, k_ * NKT * 128,
                                    [[NKT * NKT * 128, 128], [1, NKT * 128]])),
                        waits=gw, inc=16, sem="wg0" if k % 2 == 0 else "wg1")
                    psG = ps1 if k % 2 == 0 else ps6
                    w_g = [lwg, dti_m]
                    if k >= 2:
                        w_g.append(yg_marks[k - 2][2])
                    for kt in range(NKT):
                        P.emit("pe", lambda e, k2=kt, ws=gsl, p_=psG: e.matmul(
                            p_[:], w1b[:, ws.start + k2 * 128:ws.start + (k2 + 1) * 128],
                            xnT[:, k2 * TC:(k2 + 1) * TC],
                            start=(k2 == 0), stop=(k2 == NKT - 1)),
                            waits=(w_g if kt == 0 else []))
                    gmm_m = P.mark("pe")
                    if k <= 10:
                        feed_gu_mm(2 * k)
                        feed_gu_mm(2 * k + 1)
                        dn_ms = []
                    else:
                        dn_ms = list(range(3 * (k - 11), 3 * (k - 11) + 3))
                        if k == ND - 1:
                            dn_ms.append(15)
                        for m_ in dn_ms:
                            feed_dn_mm(m_)
                    for n in range(NN):
                        h = n % 2
                        brh = brep if h == 0 else war[:, 6656:6656 + TC]
                        crh = crep if h == 0 else war[:, 7168:7168 + TC]
                        dah = dAt if h == 0 else war[:, 7680:7680 + TC]
                        wb_ = [bc_hbm]
                        if len(sc_dbu) >= 2:
                            wb_.append(sc_dbu[-2])
                        lb = P.emit("gp", lambda e, n_=n, t_=t0, d=brh: e.dma_start(
                            out=d[:, 0:TC] if not isinstance(d, bass.AP) else d,
                            in_=bass.AP(bscr, n_ * S + t_, [[0, 128], [1, TC]])),
                            waits=wb_, inc=16, sem="b0" if h == 0 else "b1")
                        wc_ = [bc_hbm]
                        if len(sc_cm) >= 2:
                            wc_.append(sc_cm[-2])
                        lcp = P.emit("gp", lambda e, n_=n, t_=t0, d=crh: e.dma_start(
                            out=d[:, 0:TC] if not isinstance(d, bass.AP) else d,
                            in_=bass.AP(cscr, n_ * S + t_, [[0, 128], [1, TC]])),
                            waits=wc_, inc=16, sem="c0" if h == 0 else "c1")
                        daw = [dsp_m]
                        if len(sc_scan) >= 2:
                            daw.append(sc_scan[-2])
                        P.emit("act", lambda e, k_=k, n_=n, d_=db, o=dah: e.activation(
                            out=o[:, 0:TC] if not isinstance(o, bass.AP) else o,
                            in_=dtsp[:, d_ * TC:(d_ + 1) * TC], func=AF.Exp,
                            scale=c_A[:, k_ * NN + n_:k_ * NN + n_ + 1]),
                            waits=daw)
                        dA_m = P.mark("act")
                        P.emit("dve", lambda e, d_=db, b=brh: e.tensor_tensor(
                            dBu[:], dtx[:, d_ * TC:(d_ + 1) * TC],
                            b[:, 0:TC] if not isinstance(b, bass.AP) else b,
                            ALU.mult),
                            waits=[lb, dA_m])
                        sc_dbu.append(P.mark("dve"))
                        P.emit("dve", lambda e, k_=k, n_=n, o=dah: e.tensor_tensor_scan(
                            stt_[:], o[:, 0:TC] if not isinstance(o, bass.AP) else o,
                            dBu[:],
                            carry[:, k_ * NN + n_:k_ * NN + n_ + 1], ALU.mult, ALU.add))
                        scn_m = P.mark("dve")
                        sc_scan.append(scn_m)
                        P.emit("dve", lambda e, k_=k, n_=n: e.tensor_copy(
                            carry[:, k_ * NN + n_:k_ * NN + n_ + 1],
                            stt_[:, TC - 1:TC]), waits=[scn_m])
                        if n == 0:
                            P.emit("dve", lambda e, cc=crh: e.tensor_tensor(
                                ydt[:], stt_[:],
                                cc[:, 0:TC] if not isinstance(cc, bass.AP) else cc,
                                ALU.mult), waits=[lcp])
                            sc_cm.append(P.mark("dve"))
                        else:
                            P.emit("dve", lambda e, cc=crh: e.tensor_tensor(
                                dBu[:], stt_[:],
                                cc[:, 0:TC] if not isinstance(cc, bass.AP) else cc,
                                ALU.mult),
                                waits=[lcp])
                            sc_cm.append(P.mark("dve"))
                            P.emit("dve", lambda e: e.tensor_tensor(
                                ydt[:], ydt[:], dBu[:], ALU.add))
                    P.emit("dve", lambda e, k_=k: e.scalar_tensor_tensor(
                        ydt[:], xs_t[:, k_ * TC:(k_ + 1) * TC], c_dv[:, k_:k_ + 1],
                        ydt[:], ALU.mult, ALU.add))
                    gsw = [gmm_m] + ([yst_marks[k - 1]] if k >= 1 else [])
                    P.emit("act", lambda e, p_=psG: e.activation(
                        out=sg_t[:, 0:TC], in_=p_[:], func=AF.Silu), waits=gsw)
                    sg_m = P.mark("act")
                    P.emit("dve", lambda e: e.tensor_tensor(
                        sg_t[:, 0:TC], ydt[:], sg_t[:, 0:TC], ALU.mult),
                        waits=[sg_m])
                    ygm = P.mark("dve")
                    yst_marks.append(P.emit("sy", lambda e, k_=k: e.dma_start(
                        out=ygs.ap()[:, k_ * TC:(k_ + 1) * TC],
                        in_=sg_t[:, 0:TC]), waits=[ygm], inc=16, sem="st"))
                    if k > 10:
                        for m_ in dn_ms:
                            feed_dn_post(m_)
                    yg_marks.append((k, ygm, P.mark("act"), P.mark("pe")))
                h_done = h_marks[-1][1]
                ffst_done = P.mark("st")
                if "hffn" in taps and c == 0:
                    P.emit("gp", lambda e: e.dma_start(out=tap_t["hffn"].ap(), in_=hff[:]),
                           waits=[h_done], inc=16)
                    P.barrier()
                pe_d = P.mark("pe")
                for k in range(ND):
                    P.emit("sy", lambda e, k_=k: e.dma_start(
                        out=hff[:, k_ * TC:(k_ + 1) * TC],
                        in_=ygs.ap()[:, k_ * TC:(k_ + 1) * TC]),
                        waits=([pe_d, yst_marks[-1]] if k == 0 else []), inc=16,
                        sem="xl")
                yg_done = P.mark("xl")
                if "yg" in taps and c == 0:
                    P.emit("gp", lambda e: e.dma_start(
                        out=tap_t["yg"].ap(), in_=hff[:, 0:ND * TC]),
                        waits=[yg_done], inc=16)
                    P.barrier()

                at_ev = []
                ow_marks = {}
                omm = {}

                def ld_wob(m):
                    buf = m % 2
                    wsl = slice(buf * NKT * 128, (buf + 1) * NKT * 128)
                    lww = [dn_ev[-1][1]] if m < 2 else [omm[m - 2]]
                    ow_marks[m] = (P.emit("sy", lambda e, m_=m, ws=wsl: e.dma_start(
                        out=wob[:, ws],
                        in_=bass.AP(wo, m_ * ND * 128,
                                    [[ND * NKT * 128, 128], [1, ND * 128]])),
                        waits=lww, inc=16, sem="ow0" if m % 2 == 0 else "ow1"), wsl)

                ld_wob(0)
                ld_wob(1)
                for m in range(NKT):
                    lw, wsl = ow_marks[m]
                    psO = ps2 if m % 2 == 0 else ps3
                    w_mm = [lw, yg_done, h_done]
                    if m < 2:
                        w_mm.append(dn_ev[-1][1])
                    if m >= 2:
                        w_mm.append(at_ev[m - 2][1])
                    for kt in range(ND):
                        P.emit("pe", lambda e, k_=kt, ws=wsl, p_=psO: e.matmul(
                            p_[:], wob[:, ws.start + k_ * 128:ws.start + (k_ + 1) * 128],
                            hff[:, k_ * TC:(k_ + 1) * TC],
                            start=(k_ == 0), stop=(k_ == ND - 1)),
                            waits=(w_mm if kt == 0 else []))
                    mm_m = P.mark("pe")
                    omm[m] = mm_m
                    if m + 2 < NKT:
                        ld_wob(m + 2)
                    eb = m % 2
                    evw = [mm_m]
                    evw.append(at_ev[m - 2][2] if m >= 2 else ffst_done)
                    P.emit("dve", lambda e, p_=psO, b_=eb: e.tensor_copy(
                        ev_a[:, b_ * TC:(b_ + 1) * TC], p_[:]), waits=evw)
                    evm = P.mark("dve")
                    gpo = P.emit("gp", lambda e, m_=m, b_=eb, t_=t0: _br2(
                        e,
                        lambda e2: e2.dma_start(
                            out=bass.AP(sh_at, 0 * Hd * S + m_ * 128 * S + t_,
                                        [[S, 128], [1, TC]]),
                            in_=ev_a[:, b_ * TC:(b_ + 1) * TC]),
                        lambda e2: e2.dma_start(
                            out=bass.AP(sh_at, 1 * Hd * S + m_ * 128 * S + t_,
                                        [[S, 128], [1, TC]]),
                            in_=ev_a[:, b_ * TC:(b_ + 1) * TC]),
                        m_gp), waits=[evm], inc=16)
                    at_ev.append((m, evm, gpo))
                    if "atp" in taps and c == 0:
                        P.emit("gp", lambda e, m_=m, b_=eb: e.dma_start(
                            out=tap_t["atp"].ap()[:, m_ * TC:(m_ + 1) * TC],
                            in_=ev_a[:, b_ * TC:(b_ + 1) * TC]), waits=[evm], inc=16)
                        P.barrier()

                st_done = P.mark("st")
                st_done2 = P.mark("gp")
                P.emit("sy", lambda e, c_=c: _br2(
                    e,
                    lambda e2: e2.dma_start(out=bass.AP(sh_fl, 0, [[16, 1], [1, 1]]),
                                            in_=c_fv[0:1, c_:c_ + 1]),
                    lambda e2: e2.dma_start(out=bass.AP(sh_fl, 16, [[16, 1], [1, 1]]),
                                            in_=c_fv[0:1, c_:c_ + 1]),
                    m_sy), waits=[st_done, st_done2], inc=16)

            P.barrier()
            for t_, sh_ in (("shat", sh_at), ("shff", sh_ff)):
                if t_ in taps:
                    for m in range(NKT):
                        P.emit("gp", lambda e, tt=t_, ss=sh_, m_=m: _br2(
                            e,
                            lambda e2: e2.dma_start(
                                out=tap_t[tt].ap()[m_ * 128:(m_ + 1) * 128, :],
                                in_=bass.AP(ss, m_ * 128 * S, [[S, 128], [1, S]])),
                            lambda e2: e2.dma_start(
                                out=tap_t[tt].ap()[m_ * 128:(m_ + 1) * 128, :],
                                in_=bass.AP(ss, Hd * S + m_ * 128 * S, [[S, 128], [1, S]])),
                            m_gp), inc=16)
                    P.barrier()

            def poll(e):
                slot = _slotreg(e)
                expv = e.alloc_register("expr")
                e.reg_load(expv, c_fv[0:1, NCH - 1:NCH])
                fv = e.alloc_register("fvr")
                pc = e.alloc_register("pcr")
                cont = e.alloc_register("contr")
                e.reg_mov(fv, 0)
                e.reg_mov(pc, 0)
                e.reg_mov(cont, 1)
                with e.While(cont):
                    e.reg_add(pc, pc, 16)
                    with e.If_eq(slot, 0):
                        e.dma_start(out=fl_t[:1, :1],
                                    in_=bass.AP(sh_fl, 16, [[16, 1], [1, 1]])
                                    ).then_inc(m_poll, 16)
                    with e.Else():
                        e.dma_start(out=fl_t[:1, :1],
                                    in_=bass.AP(sh_fl, 0, [[16, 1], [1, 1]])
                                    ).then_inc(m_poll, 16)
                    e.wait_ge(m_poll, pc)
                    e.reg_load(fv, fl_t[:1, :1])
                    e.reg_alu(cont, fv, expv, ALU.not_equal)
                return None

            P.emit("gp", poll, inc=0)
            poll_m = P.mark("gp")

            out_prev = [None]
            f_am, f_ap = dAt[:, 0:TC], sg_t[:, 0:TC]
            f_fm, f_fp = dtsp[:, 0:TC], dtsp[:, TC:2 * TC]
            f_sq = stt_[:]
            f_o = ydt[:]
            f_rr = brep[:]
            f_x = xnT
            f_ss = dtx[0:1, 0:TC]
            f_rs = dtx[0:1, TC:2 * TC]

            for lc in range(2):
                P.barrier()
                xl_marks = []
                tp_ev = []
                for sub in range(NSUB):
                    def xload(e, s=sub, l=lc):
                        o0 = (l * TC + s * 128) * Hd
                        o1 = ((2 + l) * TC + s * 128) * Hd
                        return _br2(
                            e,
                            lambda e2: e2.dma_start(
                                out=xn1[:], in_=bass.AP(x_in, o0, [[Hd, 128], [1, Hd]])),
                            lambda e2: e2.dma_start(
                                out=xn1[:], in_=bass.AP(x_in, o1, [[Hd, 128], [1, Hd]])),
                            m_gp)
                    wts = [poll_m] + ([tp_ev[-1]] if tp_ev else [])
                    xl = P.emit("gp", xload, waits=wts, inc=16)
                    for kt in range(NKT):
                        pst = ps0 if (sub * NKT + kt) % 2 == 0 else ps1
                        src = bass.AP(xn1, kt * 128, [[Hd, 128], [1, 128]])
                        w = [xl]
                        if len(tp_ev) >= 2:
                            w.append(tp_ev[-2])
                        P.emit("pe", lambda e, p_=pst, s_=src: e.transpose(
                            p_[:, 0:128], s_, c_ident[:]), waits=w)
                        tpm = P.mark("pe")
                        P.emit("act", lambda e, p_=pst, k_=kt, s_=sub: e.activation(
                            out=f_x[:, k_ * TC + s_ * 128:k_ * TC + s_ * 128 + 128],
                            in_=p_[:, 0:128], func=AF.Identity), waits=[tpm])
                        tp_ev.append(P.mark("act"))
                fx_done = tp_ev[-1]
                if "xf" in taps and lc == 0:
                    P.emit("gp", lambda e: e.dma_start(out=tap_t["xf"].ap(), in_=f_x[:]),
                           waits=[fx_done], inc=16)
                    P.barrier()

                ffsum_writes = []
                sq_last = None
                for m in range(NKT):
                    P.barrier()
                    def rd_fm(e, m_=m, l=lc):
                        o0 = 0 * Hd * S + m_ * 128 * S + l * TC
                        o1 = 1 * Hd * S + m_ * 128 * S + (2 + l) * TC
                        return _br2(
                            e,
                            lambda e2: e2.dma_start(
                                out=f_fm, in_=bass.AP(sh_ff, o0, [[S, 128], [1, TC]])),
                            lambda e2: e2.dma_start(
                                out=f_fm, in_=bass.AP(sh_ff, o1, [[S, 128], [1, TC]])),
                            m_gp)

                    def rd_fp(e, m_=m, l=lc):
                        o0 = 1 * Hd * S + m_ * 128 * S + l * TC
                        o1 = 0 * Hd * S + m_ * 128 * S + (2 + l) * TC
                        return _br2(
                            e,
                            lambda e2: e2.dma_start(
                                out=f_fp, in_=bass.AP(sh_ff, o0, [[S, 128], [1, TC]])),
                            lambda e2: e2.dma_start(
                                out=f_fp, in_=bass.AP(sh_ff, o1, [[S, 128], [1, TC]])),
                            m_gp)

                    prev = P.mark("dve")
                    preva = P.mark("act")
                    w_rd = [prev, preva]
                    if ffsum_writes:
                        w_rd.append(ffsum_writes[-1])
                    r1 = P.emit("gp", rd_fm, waits=w_rd, inc=16)
                    r2 = P.emit("gp", rd_fp, inc=16)
                    P.emit("dve", lambda e: e.tensor_tensor(f_fm, f_fm, f_fp, ALU.add),
                           waits=[r1, r2])
                    fsum_m = P.mark("dve")
                    ffw = P.emit("gp", lambda e, m_=m, l=lc: e.dma_start(
                        out=ffsum.ap()[m_ * 128:(m_ + 1) * 128, l * TC:(l + 1) * TC],
                        in_=f_fm), waits=[fsum_m], inc=16)
                    ffsum_writes.append(ffw)
                    sq_w = [fsum_m]
                    if sq_last is not None:
                        sq_w.append(sq_last)
                    P.emit("act", lambda e: e.activation(out=f_sq, in_=f_fm,
                                                         func=AF.Square),
                           waits=sq_w)
                    sqm = P.mark("act")
                    P.emit("pe", lambda e, m_=m: e.matmul(
                        ps2[0:1, 0:TC], c_ones[:], f_sq,
                        start=(m_ == 0), stop=(m_ == NKT - 1)), waits=[sqm])
                    sq_last = P.mark("pe")
                P.emit("act", lambda e: e.activation(
                    out=f_rs, in_=ps2[0:1, 0:TC], func=AF.Sqrt,
                    scale=1.0 / Hd, bias=c_eps[0:1, 1:2]), waits=[sq_last])
                rs_m = P.mark("act")
                P.emit("dve", lambda e: e.reciprocal(f_ss, f_rs), waits=[rs_m])
                rr_w = P.mark("dve")
                P.emit("gp", lambda e: e.dma_start(out=bscr.ap()[0:1, 0:TC], in_=f_ss),
                       waits=[rr_w], inc=16)
                bb_m = P.mark("gp")
                P.emit("sy", lambda e: e.dma_start(
                    out=f_rr, in_=bass.AP(bscr, 0, [[0, 128], [1, TC]])),
                    waits=[bb_m], inc=16)
                rr_m = P.mark("sy")

                for m in range(NKT):
                    P.barrier()
                    def rd_am(e, m_=m, l=lc):
                        o0 = 0 * Hd * S + m_ * 128 * S + l * TC
                        o1 = 1 * Hd * S + m_ * 128 * S + (2 + l) * TC
                        return _br2(
                            e,
                            lambda e2: e2.dma_start(
                                out=f_am, in_=bass.AP(sh_at, o0, [[S, 128], [1, TC]])),
                            lambda e2: e2.dma_start(
                                out=f_am, in_=bass.AP(sh_at, o1, [[S, 128], [1, TC]])),
                            m_gp)

                    def rd_ap(e, m_=m, l=lc):
                        o0 = 1 * Hd * S + m_ * 128 * S + l * TC
                        o1 = 0 * Hd * S + m_ * 128 * S + (2 + l) * TC
                        return _br2(
                            e,
                            lambda e2: e2.dma_start(
                                out=f_ap, in_=bass.AP(sh_at, o0, [[S, 128], [1, TC]])),
                            lambda e2: e2.dma_start(
                                out=f_ap, in_=bass.AP(sh_at, o1, [[S, 128], [1, TC]])),
                            m_gp)

                    def rd_fs(e, m_=m, l=lc):
                        return e.dma_start(
                            out=f_fm,
                            in_=ffsum.ap()[m_ * 128:(m_ + 1) * 128, l * TC:(l + 1) * TC])

                    prev = P.mark("dve")
                    r1 = P.emit("gp", rd_am, waits=[prev], inc=16)
                    r2 = P.emit("gp", rd_ap, inc=16)
                    r3 = P.emit("gp", rd_fs, waits=[ffsum_writes[-1]], inc=16)
                    P.emit("dve", lambda e: e.tensor_tensor(f_am, f_am, f_ap, ALU.add),
                           waits=[r1, r2, rr_m, fx_done])
                    P.emit("dve", lambda e, m_=m: e.tensor_tensor(
                        f_am, f_am, f_x[:, m_ * TC:(m_ + 1) * TC], ALU.add))
                    P.emit("dve", lambda e: e.tensor_tensor(f_fp, f_fm, f_rr, ALU.mult),
                           waits=[r3])
                    P.emit("dve", lambda e: e.tensor_tensor(f_o, f_am, f_fp, ALU.add),
                           waits=([out_prev[0]] if out_prev[0] else []))
                    om = P.mark("dve")
                    out_prev[0] = P.emit("gp", lambda e, m_=m, l=lc: e.dma_start(
                        out=out.ap()[m_ * 128:(m_ + 1) * 128, l * TC:(l + 1) * TC],
                        in_=f_o), waits=[om], inc=16)

            last_wait = {}

            def emit_stream(eng_name, engine):
                for waits, fn, inc, sem in P.q[eng_name]:
                    for sem_name, val in waits:
                        key = (eng_name, sem_name)
                        if val > 0 and last_wait.get(key, -1) < val:
                            engine.wait_ge(SEM[sem_name], val)
                            last_wait[key] = val
                    inst = fn(engine)
                    if inc:
                        if inst == "incd":
                            pass
                        elif inst is not None:
                            inst.then_inc(SEM[sem], inc)
                        else:
                            engine.sem_inc(SEM[sem], inc)

            @block.tensor
            def _(t):
                emit_stream("pe", t)

            @block.vector
            def _(v):
                emit_stream("dve", v)

            @block.scalar
            def _(s):
                emit_stream("act", s)

            @block.sync
            def _(s):
                emit_stream("sy", s)

            @block.gpsimd
            def _(g):
                emit_stream("gp", g)
                g.wait_ge(m_gp, P.cnt["gp"])

    return nc


_GPR = {}


def _slotreg(e):
    key = id(e)
    if key not in _GPR:
        pid = e.partition_id()
        slot = e.alloc_register("slotr")
        e.reg_alu(slot, pid, 1, ALU.bitwise_and)
        _GPR[key] = slot
    return _GPR[key]


def _br2(e, fn0, fn1, incsem=None):
    slot = _slotreg(e)
    with e.If_eq(slot, 0):
        i0 = fn0(e)
        if incsem is not None and i0 is not None:
            i0.then_inc(incsem, 16)
    with e.Else():
        i1 = fn1(e)
        if incsem is not None and i1 is not None:
            i1.then_inc(incsem, 16)
    return "incd"


def kernel(**inp):
    global _LAST
    x_full = np.asarray(inp["L0_attn_hidden_states"])
    W1 = np.asarray(inp["L0_attn_in_proj_w"])
    cw = np.asarray(inp["L0_attn_conv_w"])
    cb = np.asarray(inp["L0_attn_conv_b"])
    xpw = np.asarray(inp["L0_attn_x_proj_weight"])
    dtw = np.asarray(inp["L0_attn_dt_proj_weight"])
    dtb = np.asarray(inp["L0_attn_dt_proj_bias"])
    alog = np.asarray(inp["L0_attn_A_log"])
    Dv = np.asarray(inp["L0_attn_D"])
    Wo = np.asarray(inp["L0_attn_out_proj_w"])
    gup = np.asarray(inp["L0_ffn_gate_up_packed"])
    gus = np.asarray(inp["L0_ffn_gate_up_scales"])
    dnp = np.asarray(inp["L0_ffn_down_packed"])
    dns = np.asarray(inp["L0_ffn_down_scales"])

    hperm = np.array([256 * (kt >> 1) + 2 * p + (kt & 1)
                      for kt in range(NKT) for p in range(128)])
    nonce = int.from_bytes(os.urandom(3), "little") + 7
    fv_nonce = (nonce + np.arange(1, 9, dtype=np.int64)).astype(np.int32).reshape(1, 8)
    in_maps = []
    for c in range(8):
        b, hh = c >> 1, c & 1
        isl = slice(hh * HD, (hh + 1) * HD)
        rows = 2 * np.arange(hh * HD, (hh + 1) * HD)
        in_maps.append(dict(
            x_in=np.ascontiguousarray(x_full[b]),
            w1x=np.ascontiguousarray(
                W1[rows, :].T[hperm].reshape(NKT, 128, NKT, 128)
                .transpose(1, 2, 0, 3).reshape(128, -1)),
            w1g=np.ascontiguousarray(
                W1[rows + 1, :].T[hperm].reshape(NKT, 128, NKT, 128)
                .transpose(1, 2, 0, 3).reshape(128, -1)),
            xw=np.ascontiguousarray(xpw[hh].T),
            dtw=np.ascontiguousarray(dtw[hh].T),
            dtbias=np.ascontiguousarray(dtb[hh].reshape(ND, 128).T),
            alog=np.ascontiguousarray(
                alog[hh].reshape(ND, 128, NN).transpose(1, 0, 2).reshape(128, ND * NN)),
            dvec=np.ascontiguousarray(Dv[hh].reshape(ND, 128).T),
            convw=np.ascontiguousarray(
                cw[isl].reshape(ND, 128, 4).transpose(1, 0, 2).reshape(128, ND * 4)),
            convb=np.ascontiguousarray(cb[isl].reshape(ND, 128).T),
            wo=np.ascontiguousarray(
                Wo[:, isl].T.reshape(ND, 128, NKT, 128)
                .transpose(1, 2, 0, 3).reshape(128, -1)),
            gpk=np.ascontiguousarray(
                gup[hh * OG:(hh + 1) * OG].view(np.uint8).reshape(OG, Hd // 2).T),
            upk=np.ascontiguousarray(
                gup[2 * OG + hh * OG:2 * OG + (hh + 1) * OG].view(np.uint8)
                .reshape(OG, Hd // 2).T),
            dpk=np.ascontiguousarray(
                dnp.view(np.uint8).reshape(Hd, OG)[:, hh * (OG // 2):(hh + 1) * (OG // 2)].T),
            gsc=np.ascontiguousarray(gus[:, hh * OG:(hh + 1) * OG]),
            usc=np.ascontiguousarray(gus[:, 2 * OG + hh * OG:2 * OG + (hh + 1) * OG]),
            dsc=np.ascontiguousarray(dns[hh * 22:(hh + 1) * 22]),
            fvals=fv_nonce,
        ))

    nc = build_nc(taps=_TAPS)
    res = run_bass_kernel_spmd(nc, in_maps, core_ids=list(range(8)))
    _LAST = {c: res.results[c] for c in range(8)}

    outp = np.empty((4, S, Hd), dtype=np.float32)
    for c in range(8):
        b, hh = c >> 1, c & 1
        outp[b, hh * (S // 2):(hh + 1) * (S // 2), :] = res.results[c]["out"].T
    return outp


# version 14 (final)
# speedup vs baseline: 1.0482x; 1.0482x over previous
import contextlib
import os

import numpy as np
import concourse.bass as bass
import concourse.mybir as mybir
from concourse.bass_utils import run_bass_kernel_spmd

F32 = mybir.dt.float32
BF16 = mybir.dt.bfloat16
F16 = mybir.dt.float16
U8 = mybir.dt.uint8
U16 = mybir.dt.uint16
I32 = mybir.dt.int32
AF = mybir.ActivationFunctionType
ALU = mybir.AluOpType
AXX = mybir.AxisListType.X

S = 2048
Hd = 2048
HD = 2048
TC = 512
NCH = S // TC
ND = HD // 128
NKT = Hd // 128
NN = 16
RR = 128
OG = 2816
NM2 = OG // 128
NSUB = TC // 128

ENGS = ("pe", "dve", "act", "sy", "gp")
XSEMS = ("b0", "b1", "c0", "c1", "wg0", "wg1", "gu", "dn0", "dn1",
         "ow0", "ow1", "xl", "st", "dq")

_TAPS = ()
_LAST = {}


class Prog:
    def __init__(self):
        self.q = {e: [] for e in ENGS}
        self.cnt = {e: 0 for e in ENGS}
        for s_ in XSEMS:
            self.cnt[s_] = 0
        self.pending = {e: [] for e in ENGS}

    def emit(self, eng, fn, waits=(), inc=1, sem=None):
        sem = sem or eng
        w = tuple(waits) + tuple(self.pending[eng])
        w = tuple((s, self.cnt[s]) if s not in ("pe", "dve", "act")
                  and self.cnt[s] > v else (s, v) for (s, v) in w)
        self.pending[eng] = []
        self.q[eng].append((w, fn, inc, sem))
        self.cnt[sem] += inc
        return (sem, self.cnt[sem])

    def mark(self, eng):
        return (eng, self.cnt[eng])

    def barrier(self):
        marks = {e: self.mark(e) for e in (*ENGS, *XSEMS)}
        for e in ENGS:
            self.pending[e].extend(m for k, m in marks.items() if m[1] > 0)


def build_nc(taps=()):
    nc = bass.Bass("TRN2", target_bir_lowering=False)
    taps = set(taps)
    _GPR.clear()

    x_in = nc.dram_tensor("x_in", [S, Hd], F32, kind="ExternalInput")
    w1x = nc.dram_tensor("w1x", [128, NKT * NKT * 128], F32, kind="ExternalInput")
    w1g = nc.dram_tensor("w1g", [128, NKT * NKT * 128], F32, kind="ExternalInput")
    xw = nc.dram_tensor("xw", [HD, 160], F32, kind="ExternalInput")
    dtw = nc.dram_tensor("dtw", [RR, HD], F32, kind="ExternalInput")
    dtbias = nc.dram_tensor("dtbias", [128, ND], F32, kind="ExternalInput")
    alog = nc.dram_tensor("alog", [128, ND * NN], F32, kind="ExternalInput")
    dvec = nc.dram_tensor("dvec", [128, ND], F32, kind="ExternalInput")
    convw = nc.dram_tensor("convw", [128, ND * 4], F32, kind="ExternalInput")
    convb = nc.dram_tensor("convb", [128, ND], F32, kind="ExternalInput")
    wo = nc.dram_tensor("wo", [128, ND * NKT * 128], F32, kind="ExternalInput")
    gpk = nc.dram_tensor("gpk", [Hd // 2, OG], U8, kind="ExternalInput")
    upk = nc.dram_tensor("upk", [Hd // 2, OG], U8, kind="ExternalInput")
    dpk = nc.dram_tensor("dpk", [OG // 2, Hd], U8, kind="ExternalInput")
    gsc = nc.dram_tensor("gsc", [16, OG], F16, kind="ExternalInput")
    usc = nc.dram_tensor("usc", [16, OG], F16, kind="ExternalInput")
    dsc = nc.dram_tensor("dsc", [22, Hd], F16, kind="ExternalInput")
    fvals = nc.dram_tensor("fvals", [1, 8], I32, kind="ExternalInput")
    out = nc.dram_tensor("out", [Hd, S // 2], F32, kind="ExternalOutput")

    ident_d = nc.inline_tensor(np.eye(128, dtype=np.float32), "identc")

    wg_f = nc.dram_tensor("wg_f", [128, NM2 * NKT * 128], F32)
    wu_f = nc.dram_tensor("wu_f", [128, NM2 * NKT * 128], F32)
    wd_f = nc.dram_tensor("wd_f", [128, NKT * NM2 * 128], F32)
    bscr = nc.dram_tensor("bscr", [NN, S], F32)
    cscr = nc.dram_tensor("cscr", [NN, S], F32)
    ffsum = nc.dram_tensor("ffsum", [Hd, 2 * TC], F32)
    ygs = nc.dram_tensor("ygs", [128, ND * TC], F32)
    sh_at = nc.dram_tensor("sh_at", [2, Hd, S], F32, addr_space="Shared")
    sh_ff = nc.dram_tensor("sh_ff", [2, Hd, S], F32, addr_space="Shared")
    sh_fl = nc.dram_tensor("shflag", [2, 16], I32, addr_space="Shared")

    tap_shapes = {
        "xn": [TC, Hd], "xnT": [128, NKT * TC], "xs": [128, ND * TC],
        "dtin": [RR, TC], "bmat": [16, TC], "cmat": [16, TC],
        "yg": [128, ND * TC], "hffn": [128, NM2 * TC],
        "atp": [128, NKT * TC], "ffp": [128, NKT * TC],
        "wg0": [128, OG], "wu0": [128, OG], "wd0": [128, Hd], "xf": [128, NKT * TC],
        "wgF": [Hd, OG], "wuF": [Hd, OG],
        "shat": [Hd, S], "shff": [Hd, S],
    }
    tap_t = {t: nc.dram_tensor("tap_" + t, tap_shapes[t], F32, kind="ExternalOutput")
             for t in taps}

    P = Prog()

    with contextlib.ExitStack() as _es:
        c_ident = _es.enter_context(nc.sbuf_tensor("c_ident", [128, 128], F32))
        c_dtb = _es.enter_context(nc.sbuf_tensor("c_dtb", [128, ND], F32))
        c_A = _es.enter_context(nc.sbuf_tensor("c_A", [128, ND * NN], F32))
        c_dv = _es.enter_context(nc.sbuf_tensor("c_dv", [128, ND], F32))
        c_cw = _es.enter_context(nc.sbuf_tensor("c_cw", [128, ND * 4], F32))
        c_cb = _es.enter_context(nc.sbuf_tensor("c_cb", [128, ND], F32))
        c_dtw = _es.enter_context(nc.sbuf_tensor("c_dtw", [128, HD], F32))
        c_ones = _es.enter_context(nc.sbuf_tensor("c_ones", [128, 1], F32))
        c_eps = _es.enter_context(nc.sbuf_tensor("c_eps", [128, 2], F32))
        c_fv = _es.enter_context(nc.sbuf_tensor("c_fv", [1, 8], I32))
        carry = _es.enter_context(nc.sbuf_tensor("carry", [128, ND * NN], F32))
        halo = _es.enter_context(nc.sbuf_tensor("halo", [128, ND * 4], F32))
        fl_t = _es.enter_context(nc.sbuf_tensor("fl_t", [1, 1], I32))
        xn1 = _es.enter_context(nc.sbuf_tensor("xn1", [128, Hd], F32))
        ln_s = _es.enter_context(nc.sbuf_tensor("ln_s", [128, 4], F32))
        xnT = _es.enter_context(nc.sbuf_tensor("xnT", [128, NKT * TC], F32))
        xs_t = _es.enter_context(nc.sbuf_tensor("xs_t", [128, ND * TC], F32))
        hff = _es.enter_context(nc.sbuf_tensor("hff", [128, NM2 * TC], F32))
        w1b = _es.enter_context(nc.sbuf_tensor("w1b", [128, 2 * NKT * 128], F32))
        war = _es.enter_context(nc.sbuf_tensor("war", [128, 4 * NKT * 128], F32))
        ps0 = _es.enter_context(nc.psum_tensor("ps0", [128, TC], F32))
        ps1 = _es.enter_context(nc.psum_tensor("ps1", [128, TC], F32))
        ps2 = _es.enter_context(nc.psum_tensor("ps2", [128, TC], F32))
        ps3 = _es.enter_context(nc.psum_tensor("ps3", [128, TC], F32))
        ps4 = _es.enter_context(nc.psum_tensor("ps4", [128, TC], F32))
        ps5 = _es.enter_context(nc.psum_tensor("ps5", [128, TC], F32))
        ps6 = _es.enter_context(nc.psum_tensor("ps6", [128, TC], F32))
        ps7 = _es.enter_context(nc.psum_tensor("ps7", [128, TC], F32))
        m_pe = _es.enter_context(nc.semaphore("m_pe"))
        m_dve = _es.enter_context(nc.semaphore("m_dve"))
        m_act = _es.enter_context(nc.semaphore("m_act"))
        m_sy = _es.enter_context(nc.semaphore("m_sy"))
        m_gp = _es.enter_context(nc.semaphore("m_gp"))
        m_poll = _es.enter_context(nc.semaphore("m_poll"))
        m_xs = {s_: _es.enter_context(nc.semaphore("m_" + s_)) for s_ in XSEMS}
        block = _es.enter_context(nc.Block())
        SEM = {"pe": m_pe, "dve": m_dve, "act": m_act, "sy": m_sy, "gp": m_gp}
        SEM.update(m_xs)

        c_A2 = xnT[:, 0:ND * NN]

        def ktoff(kt):
            return 256 * (kt >> 1) + (kt & 1)

        def ld(eng, dst, src):
            return P.emit(eng, lambda e, d=dst, s=src: e.dma_start(out=d, in_=s),
                          inc=16)

        ld("gp", c_ident[:], ident_d.ap())
        ld("gp", c_dtb[:], dtbias.ap())
        ld("gp", c_A2[:], alog.ap())
        ld("gp", c_dv[:], dvec.ap())
        ld("gp", c_cw[:], convw.ap())
        ld("gp", c_cb[:], convb.ap())
        ld("gp", c_dtw[:], dtw.ap())
        ld("gp", c_fv[:], fvals.ap())
        k0 = P.mark("gp")
        P.emit("act", lambda e: e.activation(out=c_A[:], in_=c_A2[:], func=AF.Exp),
               waits=[k0])
        aexp = P.mark("act")
        P.emit("dve", lambda e: e.tensor_scalar(c_A[:], c_A[:], -1.0, None, ALU.mult),
               waits=[aexp])
        P.emit("dve", lambda e: e.memset(carry[:], 0.0))
        P.emit("dve", lambda e: e.memset(halo[:], 0.0))
        P.emit("dve", lambda e: e.memset(c_ones[:], 1.0))
        P.emit("dve", lambda e: e.memset(c_eps[:, 0:1], 1e-5))
        P.emit("dve", lambda e: e.memset(c_eps[:, 1:2], 1e-6))

        OH = 768

        _dq = contextlib.ExitStack()
        q_b8 = _dq.enter_context(nc.sbuf_tensor("q_b8", [128, OH], U8))
        q_w16 = _dq.enter_context(nc.sbuf_tensor("q_w16", [128, OH], U16))
        q_nib = _dq.enter_context(nc.sbuf_tensor("q_nib", [128, OH], U16))
        q_t1 = _dq.enter_context(nc.sbuf_tensor("q_t1", [128, OH], U16))
        q_t2 = _dq.enter_context(nc.sbuf_tensor("q_t2", [128, OH], U16))
        q_v1 = _dq.enter_context(nc.sbuf_tensor("q_v1", [128, OH], BF16))
        q_s16 = _dq.enter_context(nc.sbuf_tensor("q_s16", [128, OH], F16))
        q_s32 = _dq.enter_context(nc.sbuf_tensor("q_s32", [128, OH], F32))
        q_p = _dq.enter_context(nc.sbuf_tensor("q_p", [128, 2 * OH], F32))
        if True:
            s32_prev = [None]

            def dequant(src_pk, src_sc, n_blk, ocols, dst, perm_cols, mstride):
                if perm_cols:
                    ohs = [(0, 768), (768, 768), (1536, 768), (2304, 512)]
                else:
                    ohs = [(0, 768), (768, 768), (1536, 512)]
                for b_ in range(n_blk):
                    for (obase, oc) in ohs:
                        l0 = ld("sy", q_b8[:, :oc],
                                src_pk.ap()[b_ * 128:(b_ + 1) * 128, obase:obase + oc])
                        src_s = bass.AP(src_sc, 2 * b_ * ocols + obase,
                                        [[ocols, 2], [0, 64], [1, oc]])
                        l1 = P.emit("sy", lambda e, s=src_s, c_=oc: e.dma_start(
                            out=q_s16[:, :c_], in_=s),
                            waits=([s32_prev[0]] if s32_prev[0] else []), inc=16)
                        P.emit("dve", lambda e, c_=oc: e.tensor_copy(
                            q_w16[:, :c_], q_b8[:, :c_]), waits=[l0])
                        P.emit("dve", lambda e, c_=oc: e.tensor_copy(
                            q_s32[:, :c_], q_s16[:, :c_]), waits=[l1])
                        s32_prev[0] = P.mark("dve")
                        for par in range(2):
                            if par == 0:
                                P.emit("dve", lambda e, c_=oc: e.tensor_scalar(
                                    q_nib[:, :c_], q_w16[:, :c_], 15, None,
                                    ALU.bitwise_and))
                            else:
                                P.emit("dve", lambda e, c_=oc: e.tensor_scalar(
                                    q_nib[:, :c_], q_w16[:, :c_], 4, 15,
                                    ALU.logical_shift_right, ALU.bitwise_and))
                            P.emit("dve", lambda e, c_=oc: e.tensor_scalar(
                                q_t1[:, :c_], q_nib[:, :c_], 7, 6,
                                ALU.bitwise_and, ALU.logical_shift_left))
                            P.emit("dve", lambda e, c_=oc: e.tensor_scalar(
                                q_t2[:, :c_], q_nib[:, :c_], 8, 12,
                                ALU.bitwise_and, ALU.logical_shift_left))
                            P.emit("dve", lambda e, c_=oc: e.tensor_tensor(
                                q_t1[:, :c_], q_t1[:, :c_], q_t2[:, :c_],
                                ALU.bitwise_or))
                            P.emit("dve", lambda e, c_=oc: e.tensor_scalar(
                                q_v1[:, :c_], q_t1[:, :c_].bitcast(BF16),
                                float(2.0 ** 63), float(2.0 ** 63),
                                ALU.mult, ALU.mult))
                            if perm_cols:
                                nb = oc // 256
                                po = bass.AP(q_p, par * OH,
                                             [[2 * OH, 128], [256, nb], [1, 128], [128, 2]])
                                pv = bass.AP(q_v1, 0,
                                             [[OH, 128], [256, nb], [2, 128], [1, 2]])
                                psc = bass.AP(q_s32, 0,
                                              [[OH, 128], [256, nb], [2, 128], [1, 2]])
                                P.emit("dve", lambda e, a=po, v=pv, s=psc:
                                       e.tensor_tensor(a, v, s, ALU.mult))
                            else:
                                P.emit("dve", lambda e, p_=par, c_=oc: e.tensor_tensor(
                                    q_p[:, p_ * OH:p_ * OH + c_], q_v1[:, :c_],
                                    q_s32[:, :c_], ALU.mult))
                        dv = P.mark("dve")
                        rl = n_blk * 2 * 128 * mstride // 128 if False else None
                        for par in range(2):
                            kt2 = None
                            P.emit("sy", lambda e, b2=b_, p2=par, ob=obase, c_=oc,
                                   d2=dst, ms=mstride: e.dma_start(
                                       out=bass.AP(
                                           d2, (ob // 128) * ms + (2 * b2 + p2) * 128,
                                           [[NM2 * NKT * 128, 128],
                                            [ms, c_ // 128], [1, 128]]),
                                       in_=q_p[:, p2 * OH:p2 * OH + c_]),
                                   waits=[dv], inc=16, sem="dq")

            dequant(gpk, gsc, 8, OG, wg_f, True, NKT * 128)
            dequant(upk, usc, 8, OG, wu_f, True, NKT * 128)
            dequant(dpk, dsc, 11, Hd, wd_f, False, NM2 * 128)
        _dq.close()
        deq_gp = P.mark("dq")
        deq_dve = P.mark("dve")

        for t, d2 in (("wg0", wg_f), ("wu0", wu_f), ("wd0", wd_f)):
            if t in taps:
                P.emit("gp", lambda e, t_=t, s=d2: e.dma_start(
                    out=tap_t[t_].ap(), in_=s.ap()[0:128, :]),
                    waits=[deq_gp], inc=16)
        for t, d2 in (("wgF", wg_f), ("wuF", wu_f)):
            if t in taps:
                P.emit("gp", lambda e, t_=t, s=d2: e.dma_start(
                    out=tap_t[t_].ap(), in_=s.ap()),
                    waits=[deq_gp], inc=16)
        if taps:
            P.barrier()

        xpj = _es.enter_context(nc.sbuf_tensor("xpj", [128, TC + 3], F32))
        dtin = _es.enter_context(nc.sbuf_tensor("dtin", [128, TC], F32))
        dtsp = _es.enter_context(nc.sbuf_tensor("dtsp", [128, 2 * TC], F32))
        dtx = _es.enter_context(nc.sbuf_tensor("dtx", [128, 2 * TC], F32))
        brep = _es.enter_context(nc.sbuf_tensor("brep", [128, TC], F32))
        crep = _es.enter_context(nc.sbuf_tensor("crep", [128, TC], F32))
        dAt = _es.enter_context(nc.sbuf_tensor("dAt", [128, TC], F32))
        dBu = _es.enter_context(nc.sbuf_tensor("dBu", [128, TC], F32))
        stt_ = _es.enter_context(nc.sbuf_tensor("stt_", [128, TC], F32))
        tmp_ = _es.enter_context(nc.sbuf_tensor("tmp_", [128, TC], F32))
        ydt = _es.enter_context(nc.sbuf_tensor("ydt", [128, TC], F32))
        sg_t = _es.enter_context(nc.sbuf_tensor("sg_t", [128, TC], F32))
        ev_a = _es.enter_context(nc.sbuf_tensor("ev_a", [128, 2 * TC], F32))
        if True:
            c_xw = war[:, 5632:5632 + ND * 160]
            bc_s = sg_t
            wgb = war[:, 0:NKT * 128]
            wub = war[:, NKT * 128:2 * NKT * 128]
            wdb = war[:, 0:2 * NM2 * 128]
            wob = war[:, 0:2 * NKT * 128]
            z_sp = tmp_
            for c in range(NCH):
                t0 = c * TC
                P.barrier()

                ev_done = []
                for sub in range(NSUB):
                    lx = P.emit("gp", lambda e, s=sub, t_=t0: e.dma_start(
                        out=xn1[:], in_=x_in.ap()[t_ + s * 128:t_ + (s + 1) * 128, :]),
                        waits=([ev_done[-1]] if ev_done else []), inc=16, sem="xl")
                    P.emit("dve", lambda e: e.reduce_sum(ln_s[:, 0:1], xn1[:], axis=AXX),
                           waits=[lx])
                    rdc_m = P.mark("dve")
                    P.emit("dve", lambda e: e.tensor_scalar(
                        ln_s[:, 1:2], ln_s[:, 0:1], 1.0 / Hd, None, ALU.mult),
                        waits=[rdc_m])
                    mu_m = P.mark("dve")
                    P.emit("dve", lambda e: e.tensor_scalar(
                        xn1[:], xn1[:], ln_s[:, 1:2], None, ALU.subtract),
                        waits=[mu_m])
                    xc_m = P.mark("dve")
                    P.emit("act", lambda e: e.activation(
                        out=xs_t[:, 0:Hd], in_=xn1[:],
                        func=AF.Square, accum_out=ln_s[:, 2:3]), waits=[xc_m])
                    sq_m = P.mark("act")
                    P.emit("act", lambda e: e.activation(
                        out=ln_s[:, 3:4], in_=ln_s[:, 2:3], func=AF.Sqrt,
                        scale=1.0 / Hd, bias=c_eps[:, 0:1]), waits=[sq_m])
                    sd_m = P.mark("act")
                    P.emit("dve", lambda e: e.reciprocal(ln_s[:, 2:3], ln_s[:, 3:4]),
                           waits=[sd_m])
                    rs_self = P.mark("dve")
                    P.emit("dve", lambda e: e.tensor_scalar(
                        xn1[:], xn1[:], ln_s[:, 2:3], None, ALU.mult),
                        waits=[rs_self])
                    xn_m = P.mark("dve")
                    if "xn" in taps and c == 0:
                        P.emit("gp", lambda e, s=sub: e.dma_start(
                            out=tap_t["xn"].ap()[s * 128:(s + 1) * 128, :], in_=xn1[:]),
                            waits=[xn_m], inc=16)
                        P.barrier()
                    for kt in range(NKT):
                        pst = ps0 if (sub * NKT + kt) % 2 == 0 else ps1
                        src = bass.AP(xn1, ktoff(kt), [[Hd, 128], [2, 128]])
                        w = [xn_m]
                        if len(ev_done) >= 2:
                            w.append(ev_done[-2])
                        P.emit("pe", lambda e, p_=pst, s_=src: e.transpose(
                            p_[:, 0:128], s_, c_ident[:]), waits=w)
                        tp_m = P.mark("pe")
                        P.emit("act", lambda e, p_=pst, k_=kt, s_=sub: e.activation(
                            out=xnT[:, k_ * TC + s_ * 128:k_ * TC + s_ * 128 + 128],
                            in_=p_[:, 0:128], func=AF.Identity), waits=[tp_m])
                        ev_done.append(P.mark("act"))
                xnT_done = ev_done[-1]
                if "xnT" in taps and c == 0:
                    P.emit("gp", lambda e: e.dma_start(out=tap_t["xnT"].ap(), in_=xnT[:]),
                           waits=[xnT_done], inc=16)
                    P.barrier()

                for k in range(ND):
                    P.emit("sy", lambda e, k_=k: e.dma_start(
                        out=c_xw[:, k_ * 160:(k_ + 1) * 160],
                        in_=xw.ap()[k_ * 128:(k_ + 1) * 128, :]), inc=16, sem="gu")
                cxw_m = P.mark("gu")
                conv_marks = []
                for k in range(ND):
                    buf = k % 2
                    wsl = slice(buf * NKT * 128, (buf + 1) * NKT * 128)
                    lw = P.emit("gp", lambda e, k_=k, ws=wsl: e.dma_start(
                        out=w1b[:, ws],
                        in_=bass.AP(w1x, k_ * NKT * 128,
                                    [[NKT * NKT * 128, 128], [1, NKT * 128]])),
                        waits=([conv_marks[k - 2][2]] if k >= 2 else []), inc=16,
                        sem="wg0" if k % 2 == 0 else "wg1")
                    psD = ps2 if k % 2 == 0 else ps3
                    w_mm = [lw, xnT_done]
                    if k >= 2:
                        w_mm.append(conv_marks[k - 2][1])
                    for kt in range(NKT):
                        P.emit("pe", lambda e, k2=kt, ws=wsl, p_=psD: e.matmul(
                            p_[:], w1b[:, ws.start + k2 * 128:ws.start + (k2 + 1) * 128],
                            xnT[:, k2 * TC:(k2 + 1) * TC],
                            start=(k2 == 0), stop=(k2 == NKT - 1)),
                            waits=(w_mm if kt == 0 else []))
                    mm_m = P.mark("pe")
                    xo = 0
                    wev = [mm_m]
                    if c == 0 and k == 0:
                        wev += [deq_dve, deq_gp]
                    P.emit("dve", lambda e, p_=psD, o=xo: e.tensor_copy(
                        xpj[:, o + 3:o + 3 + TC], p_[:]), waits=wev)
                    P.emit("dve", lambda e, k_=k, o=xo: e.tensor_copy(
                        xpj[:, o:o + 3], halo[:, k_ * 4:k_ * 4 + 3]))
                    P.emit("dve", lambda e, k_=k, o=xo: e.tensor_copy(
                        halo[:, k_ * 4:k_ * 4 + 3], xpj[:, o + TC:o + TC + 3]))
                    hlo_m = P.mark("dve")
                    P.emit("dve", lambda e, k_=k, o=xo: e.tensor_scalar(
                        tmp_[:, 0:TC], xpj[:, o:o + TC],
                        c_cw[:, k_ * 4:k_ * 4 + 1], None, ALU.mult),
                        waits=[hlo_m])
                    for kk in range(1, 4):
                        P.emit("dve", lambda e, k_=k, o=xo, q=kk: e.scalar_tensor_tensor(
                            tmp_[:, 0:TC], xpj[:, o + q:o + q + TC],
                            c_cw[:, k_ * 4 + q:k_ * 4 + q + 1], tmp_[:, 0:TC],
                            ALU.mult, ALU.add))
                    cm = P.mark("dve")
                    P.emit("act", lambda e, k_=k: e.activation(
                        out=xs_t[:, k_ * TC:(k_ + 1) * TC], in_=tmp_[:, 0:TC],
                        func=AF.Silu, bias=c_cb[:, k_:k_ + 1]), waits=[cm])
                    xs_m = P.mark("act")
                    conv_marks.append((k, xs_m, P.mark("pe")))
                for k in range(ND):
                    w_xp = [conv_marks[k][1]] + ([cxw_m] if k == 0 else [])
                    P.emit("pe", lambda e, k_=k: e.matmul(
                        ps0[:, 0:TC], c_xw[:, k_ * 160:k_ * 160 + 128],
                        xs_t[:, k_ * TC:(k_ + 1) * TC],
                        start=(k_ == 0), stop=(k_ == ND - 1)), waits=w_xp)
                    P.emit("pe", lambda e, k_=k: e.matmul(
                        ps1[0:32, 0:TC], c_xw[:, k_ * 160 + 128:(k_ + 1) * 160],
                        xs_t[:, k_ * TC:(k_ + 1) * TC],
                        start=(k_ == 0), stop=(k_ == ND - 1)))
                xp_done = P.mark("pe")
                if "xs" in taps and c == 0:
                    P.emit("gp", lambda e: e.dma_start(out=tap_t["xs"].ap(), in_=xs_t[:]),
                           waits=[conv_marks[-1][1]], inc=16)
                    P.barrier()

                P.emit("dve", lambda e: e.tensor_copy(dtin[:], ps0[:, 0:TC]),
                       waits=[xp_done])
                P.emit("dve", lambda e: e.tensor_copy(bc_s[0:32, 0:TC], ps1[0:32, 0:TC]))
                bc_m = P.mark("dve")
                P.emit("gp", lambda e, t_=t0: e.dma_start(
                    out=bscr.ap()[:, t_:t_ + TC], in_=bc_s[0:16, 0:TC]),
                    waits=[bc_m], inc=16)
                P.emit("gp", lambda e, t_=t0: e.dma_start(
                    out=cscr.ap()[:, t_:t_ + TC], in_=bc_s[16:32, 0:TC]),
                    waits=[bc_m], inc=16)
                bc_hbm = P.mark("gp")
                for t_, sl_ in (("dtin", None), ("bmat", (0, 16)), ("cmat", (16, 32))):
                    if t_ in taps and c == 0:
                        srcap = dtin[:] if sl_ is None else bc_s[sl_[0]:sl_[1], 0:TC]
                        P.emit("gp", lambda e, a=tap_t[t_], s=srcap: e.dma_start(
                            out=a.ap(), in_=s), waits=[bc_m], inc=16)
                        P.barrier()

                dti_m = P.mark("dve")
                h_marks = []
                dn_ev = []
                gu_mm = {}
                gu_g = {}
                dn_mm = {}

                def feed_gu_mm(m):
                    wwait = [gu_g[m - 1]] if m >= 1 else [dti_m]
                    lg = P.emit("sy", lambda e, m_=m: e.dma_start(
                        out=wgb[:],
                        in_=bass.AP(wg_f, m_ * NKT * 128,
                                    [[NM2 * NKT * 128, 128], [1, NKT * 128]])),
                        waits=wwait + [deq_gp], inc=16, sem="gu")
                    ub = m % 2
                    usl = slice((1 + ub) * NKT * 128, (2 + ub) * NKT * 128)
                    uw_ = [gu_mm[m - 2][0]] if m >= 2 else [dti_m]
                    lu = P.emit("sy", lambda e, m_=m, us=usl: e.dma_start(
                        out=war[:, us],
                        in_=bass.AP(wu_f, m_ * NKT * 128,
                                    [[NM2 * NKT * 128, 128], [1, NKT * 128]])),
                        waits=uw_, inc=16, sem="dn0" if ub == 0 else "dn1")
                    psA, psB = (ps2, ps3) if m % 2 == 0 else (ps4, ps5)
                    w_mm = [lg, xnT_done]
                    w_mm.append(h_marks[m - 2][1] if m >= 2 else dti_m)
                    for kt in range(NKT):
                        P.emit("pe", lambda e, k_=kt, pA=psA: e.matmul(
                            pA[:], wgb[:, k_ * 128:(k_ + 1) * 128],
                            xnT[:, k_ * TC:(k_ + 1) * TC],
                            start=(k_ == 0), stop=(k_ == NKT - 1)),
                            waits=(w_mm if kt == 0 else []))
                    gu_g[m] = P.mark("pe")
                    for kt in range(NKT):
                        P.emit("pe", lambda e, k_=kt, pB=psB, us=usl: e.matmul(
                            pB[:], war[:, us.start + k_ * 128:us.start + (k_ + 1) * 128],
                            xnT[:, k_ * TC:(k_ + 1) * TC],
                            start=(k_ == 0), stop=(k_ == NKT - 1)),
                            waits=([lu] if kt == 0 else []))
                    gu_mm[m] = (P.mark("pe"), psA, psB)

                def feed_gu_post(m):
                    mm_m, psA, psB = gu_mm[m]
                    sw = [mm_m] + ([h_marks[m - 1][1]] if m >= 1 else [])
                    P.emit("act", lambda e, pA=psA: e.activation(
                        out=xpj[:, 0:TC], in_=pA[:], func=AF.Silu), waits=sw)
                    sg_m = P.mark("act")
                    P.emit("dve", lambda e, m_=m, pB=psB: e.tensor_tensor(
                        hff[:, m_ * TC:(m_ + 1) * TC], xpj[:, 0:TC],
                        pB[:], ALU.mult), waits=[sg_m])
                    h_marks.append((m, P.mark("dve")))

                DNB = [ps7, ps2, ps3, ps4, ps5]

                def feed_dn_mm(m):
                    buf = m % 2
                    wsl = slice(buf * NM2 * 128, (buf + 1) * NM2 * 128)
                    lw = P.emit("sy", lambda e, m_=m, ws=wsl: e.dma_start(
                        out=wdb[:, ws],
                        in_=bass.AP(wd_f, m_ * NM2 * 128,
                                    [[NKT * NM2 * 128, 128], [1, NM2 * 128]])),
                        waits=([dn_mm[m - 2][0]] if m >= 2 else [h_marks[-1][1]]),
                        inc=16, sem="dn0" if m % 2 == 0 else "dn1")
                    psC = DNB[m % 5]
                    w_mm = [lw, h_marks[-1][1]]
                    if m >= 5:
                        w_mm.append(dn_ev[m - 5][1])
                    for kt in range(NM2):
                        P.emit("pe", lambda e, k_=kt, ws=wsl, p_=psC: e.matmul(
                            p_[:], wdb[:, ws.start + k_ * 128:ws.start + (k_ + 1) * 128],
                            hff[:, k_ * TC:(k_ + 1) * TC],
                            start=(k_ == 0), stop=(k_ == NM2 - 1)),
                            waits=(w_mm if kt == 0 else []))
                    dn_mm[m] = (P.mark("pe"), psC)

                def feed_dn_post(m):
                    mm_m, psC = dn_mm[m]
                    eb = m % 2
                    evw = [mm_m] + ([dn_ev[m - 2][2]] if m >= 2 else [])
                    P.emit("dve", lambda e, p_=psC, b_=eb: e.tensor_copy(
                        ev_a[:, b_ * TC:(b_ + 1) * TC], p_[:]), waits=evw)
                    evm = P.mark("dve")
                    gpw = P.emit("sy", lambda e, m_=m, b_=eb, t_=t0: _br2(
                        e,
                        lambda e2: e2.dma_start(
                            out=bass.AP(sh_ff, 0 * Hd * S + m_ * 128 * S + t_,
                                        [[S, 128], [1, TC]]),
                            in_=ev_a[:, b_ * TC:(b_ + 1) * TC]),
                        lambda e2: e2.dma_start(
                            out=bass.AP(sh_ff, 1 * Hd * S + m_ * 128 * S + t_,
                                        [[S, 128], [1, TC]]),
                            in_=ev_a[:, b_ * TC:(b_ + 1) * TC]),
                        m_xs["st"]), waits=[evm], inc=16, sem="st")
                    dn_ev.append((m, evm, gpw))
                    if "ffp" in taps and c == 0:
                        P.emit("gp", lambda e, m_=m, b_=eb: e.dma_start(
                            out=tap_t["ffp"].ap()[:, m_ * TC:(m_ + 1) * TC],
                            in_=ev_a[:, b_ * TC:(b_ + 1) * TC]), waits=[evm], inc=16)
                        P.barrier()

                yg_marks = []
                yst_marks = []
                zsp_marks = []
                dsp_marks = []
                sc_dbu, sc_cm, sc_scan = [], [], []
                for k in range(ND):
                    if 1 <= k <= 11:
                        feed_gu_post(2 * (k - 1))
                        feed_gu_post(2 * (k - 1) + 1)
                    db = k % 2
                    w_dt = [dti_m]
                    if k >= 1:
                        w_dt.append(zsp_marks[k - 1])
                    P.emit("pe", lambda e, k_=k: e.matmul(
                        ps0[:, 0:TC], c_dtw[:, k_ * 128:(k_ + 1) * 128], dtin[:],
                        start=True, stop=True), waits=w_dt)
                    dtp_m = P.mark("pe")
                    zw = [dtp_m]
                    if k >= 1:
                        zw.append(dsp_marks[k - 1])
                    P.emit("act", lambda e, k_=k: e.activation(
                        out=z_sp[:, 0:TC], in_=ps0[:, 0:TC], func=AF.Identity,
                        bias=c_dtb[:, k_:k_ + 1]), waits=zw)
                    zsp_marks.append(P.mark("act"))
                    aw = [dsp_marks[k - 2]] if k >= 2 else []
                    P.emit("act", lambda e, b_=db: e.activation(
                        out=dtsp[:, b_ * TC:(b_ + 1) * TC], in_=z_sp[:, 0:TC],
                        func=AF.Abs), waits=aw)
                    P.emit("act", lambda e, b_=db: e.activation(
                        out=dtsp[:, b_ * TC:(b_ + 1) * TC],
                        in_=dtsp[:, b_ * TC:(b_ + 1) * TC], func=AF.Exp, scale=-1.0))
                    P.emit("act", lambda e, b_=db: e.activation(
                        out=dtsp[:, b_ * TC:(b_ + 1) * TC],
                        in_=dtsp[:, b_ * TC:(b_ + 1) * TC], func=AF.Ln, bias=1.0))
                    spl_m = P.mark("act")
                    P.emit("dve", lambda e, b_=db: e.scalar_tensor_tensor(
                        dtsp[:, b_ * TC:(b_ + 1) * TC], z_sp[:, 0:TC], 0.0,
                        dtsp[:, b_ * TC:(b_ + 1) * TC], ALU.max, ALU.add),
                        waits=[spl_m])
                    P.emit("dve", lambda e, k_=k, b_=db: e.tensor_tensor(
                        dtx[:, b_ * TC:(b_ + 1) * TC], dtsp[:, b_ * TC:(b_ + 1) * TC],
                        xs_t[:, k_ * TC:(k_ + 1) * TC], ALU.mult))
                    dsp_m = P.mark("dve")
                    dsp_marks.append(dsp_m)
                    gbuf = k % 2
                    gsl = slice(gbuf * NKT * 128, (gbuf + 1) * NKT * 128)
                    gw = [xp_done] if k < 2 else [yg_marks[k - 2][3]]
                    lwg = P.emit("sy", lambda e, k_=k, ws=gsl: e.dma_start(
                        out=w1b[:, ws],
                        in_=bass.AP(w1g, k_ * NKT * 128,
                                    [[NKT * NKT * 128, 128], [1, NKT * 128]])),
                        waits=gw, inc=16, sem="wg0" if k % 2 == 0 else "wg1")
                    psG = ps1 if k % 2 == 0 else ps6
                    w_g = [lwg, dti_m]
                    if k >= 2:
                        w_g.append(yg_marks[k - 2][2])
                    for kt in range(NKT):
                        P.emit("pe", lambda e, k2=kt, ws=gsl, p_=psG: e.matmul(
                            p_[:], w1b[:, ws.start + k2 * 128:ws.start + (k2 + 1) * 128],
                            xnT[:, k2 * TC:(k2 + 1) * TC],
                            start=(k2 == 0), stop=(k2 == NKT - 1)),
                            waits=(w_g if kt == 0 else []))
                    gmm_m = P.mark("pe")
                    if k <= 10:
                        feed_gu_mm(2 * k)
                        feed_gu_mm(2 * k + 1)
                        dn_ms = []
                    else:
                        dn_ms = list(range(3 * (k - 11), 3 * (k - 11) + 3))
                        if k == ND - 1:
                            dn_ms.append(15)
                        for m_ in dn_ms:
                            feed_dn_mm(m_)
                    for n in range(NN):
                        h = n % 2
                        brh = brep if h == 0 else war[:, 6656:6656 + TC]
                        crh = crep if h == 0 else war[:, 7168:7168 + TC]
                        dah = dAt if h == 0 else war[:, 7680:7680 + TC]
                        wb_ = [bc_hbm]
                        if len(sc_dbu) >= 2:
                            wb_.append(sc_dbu[-2])
                        lb = P.emit("gp", lambda e, n_=n, t_=t0, d=brh: e.dma_start(
                            out=d[:, 0:TC] if not isinstance(d, bass.AP) else d,
                            in_=bass.AP(bscr, n_ * S + t_, [[0, 128], [1, TC]])),
                            waits=wb_, inc=16, sem="b0" if h == 0 else "b1")
                        wc_ = [bc_hbm]
                        if len(sc_cm) >= 2:
                            wc_.append(sc_cm[-2])
                        lcp = P.emit("gp", lambda e, n_=n, t_=t0, d=crh: e.dma_start(
                            out=d[:, 0:TC] if not isinstance(d, bass.AP) else d,
                            in_=bass.AP(cscr, n_ * S + t_, [[0, 128], [1, TC]])),
                            waits=wc_, inc=16, sem="c0" if h == 0 else "c1")
                        daw = [dsp_m]
                        if len(sc_scan) >= 2:
                            daw.append(sc_scan[-2])
                        P.emit("act", lambda e, k_=k, n_=n, d_=db, o=dah: e.activation(
                            out=o[:, 0:TC] if not isinstance(o, bass.AP) else o,
                            in_=dtsp[:, d_ * TC:(d_ + 1) * TC], func=AF.Exp,
                            scale=c_A[:, k_ * NN + n_:k_ * NN + n_ + 1]),
                            waits=daw)
                        dA_m = P.mark("act")
                        P.emit("dve", lambda e, d_=db, b=brh: e.tensor_tensor(
                            dBu[:], dtx[:, d_ * TC:(d_ + 1) * TC],
                            b[:, 0:TC] if not isinstance(b, bass.AP) else b,
                            ALU.mult),
                            waits=[lb, dA_m])
                        sc_dbu.append(P.mark("dve"))
                        P.emit("dve", lambda e, k_=k, n_=n, o=dah: e.tensor_tensor_scan(
                            stt_[:], o[:, 0:TC] if not isinstance(o, bass.AP) else o,
                            dBu[:],
                            carry[:, k_ * NN + n_:k_ * NN + n_ + 1], ALU.mult, ALU.add))
                        scn_m = P.mark("dve")
                        sc_scan.append(scn_m)
                        P.emit("dve", lambda e, k_=k, n_=n: e.tensor_copy(
                            carry[:, k_ * NN + n_:k_ * NN + n_ + 1],
                            stt_[:, TC - 1:TC]), waits=[scn_m])
                        if n == 0:
                            P.emit("dve", lambda e, cc=crh: e.tensor_tensor(
                                ydt[:], stt_[:],
                                cc[:, 0:TC] if not isinstance(cc, bass.AP) else cc,
                                ALU.mult), waits=[lcp])
                            sc_cm.append(P.mark("dve"))
                        else:
                            P.emit("dve", lambda e, cc=crh: e.tensor_tensor(
                                dBu[:], stt_[:],
                                cc[:, 0:TC] if not isinstance(cc, bass.AP) else cc,
                                ALU.mult),
                                waits=[lcp])
                            sc_cm.append(P.mark("dve"))
                            P.emit("dve", lambda e: e.tensor_tensor(
                                ydt[:], ydt[:], dBu[:], ALU.add))
                    P.emit("dve", lambda e, k_=k: e.scalar_tensor_tensor(
                        ydt[:], xs_t[:, k_ * TC:(k_ + 1) * TC], c_dv[:, k_:k_ + 1],
                        ydt[:], ALU.mult, ALU.add))
                    gsw = [gmm_m] + ([yst_marks[k - 1]] if k >= 1 else [])
                    P.emit("act", lambda e, p_=psG: e.activation(
                        out=sg_t[:, 0:TC], in_=p_[:], func=AF.Silu), waits=gsw)
                    sg_m = P.mark("act")
                    P.emit("dve", lambda e: e.tensor_tensor(
                        sg_t[:, 0:TC], ydt[:], sg_t[:, 0:TC], ALU.mult),
                        waits=[sg_m])
                    ygm = P.mark("dve")
                    yst_marks.append(P.emit("sy", lambda e, k_=k: e.dma_start(
                        out=ygs.ap()[:, k_ * TC:(k_ + 1) * TC],
                        in_=sg_t[:, 0:TC]), waits=[ygm], inc=16, sem="st"))
                    if k > 10:
                        for m_ in dn_ms:
                            feed_dn_post(m_)
                    yg_marks.append((k, ygm, P.mark("act"), P.mark("pe")))
                h_done = h_marks[-1][1]
                ffst_done = P.mark("st")
                if "hffn" in taps and c == 0:
                    P.emit("gp", lambda e: e.dma_start(out=tap_t["hffn"].ap(), in_=hff[:]),
                           waits=[h_done], inc=16)
                    P.barrier()
                pe_d = P.mark("pe")
                for k in range(ND):
                    P.emit("sy", lambda e, k_=k: e.dma_start(
                        out=hff[:, k_ * TC:(k_ + 1) * TC],
                        in_=ygs.ap()[:, k_ * TC:(k_ + 1) * TC]),
                        waits=([pe_d, yst_marks[-1]] if k == 0 else []), inc=16,
                        sem="xl")
                yg_done = P.mark("xl")
                if "yg" in taps and c == 0:
                    P.emit("gp", lambda e: e.dma_start(
                        out=tap_t["yg"].ap(), in_=hff[:, 0:ND * TC]),
                        waits=[yg_done], inc=16)
                    P.barrier()

                at_ev = []
                ow_marks = {}
                omm = {}

                def ld_wob(m):
                    buf = m % 2
                    wsl = slice(buf * NKT * 128, (buf + 1) * NKT * 128)
                    lww = [dn_ev[-1][1]] if m < 2 else [omm[m - 2]]
                    ow_marks[m] = (P.emit("sy", lambda e, m_=m, ws=wsl: e.dma_start(
                        out=wob[:, ws],
                        in_=bass.AP(wo, m_ * ND * 128,
                                    [[ND * NKT * 128, 128], [1, ND * 128]])),
                        waits=lww, inc=16, sem="ow0" if m % 2 == 0 else "ow1"), wsl)

                ld_wob(0)
                ld_wob(1)
                for m in range(NKT):
                    lw, wsl = ow_marks[m]
                    psO = ps2 if m % 2 == 0 else ps3
                    w_mm = [lw, yg_done, h_done]
                    if m < 2:
                        w_mm.append(dn_ev[-1][1])
                    if m >= 2:
                        w_mm.append(at_ev[m - 2][1])
                    for kt in range(ND):
                        P.emit("pe", lambda e, k_=kt, ws=wsl, p_=psO: e.matmul(
                            p_[:], wob[:, ws.start + k_ * 128:ws.start + (k_ + 1) * 128],
                            hff[:, k_ * TC:(k_ + 1) * TC],
                            start=(k_ == 0), stop=(k_ == ND - 1)),
                            waits=(w_mm if kt == 0 else []))
                    mm_m = P.mark("pe")
                    omm[m] = mm_m
                    if m + 2 < NKT:
                        ld_wob(m + 2)
                    eb = m % 2
                    evw = [mm_m]
                    evw.append(at_ev[m - 2][2] if m >= 2 else ffst_done)
                    P.emit("dve", lambda e, p_=psO, b_=eb: e.tensor_copy(
                        ev_a[:, b_ * TC:(b_ + 1) * TC], p_[:]), waits=evw)
                    evm = P.mark("dve")
                    gpo = P.emit("gp", lambda e, m_=m, b_=eb, t_=t0: _br2(
                        e,
                        lambda e2: e2.dma_start(
                            out=bass.AP(sh_at, 0 * Hd * S + m_ * 128 * S + t_,
                                        [[S, 128], [1, TC]]),
                            in_=ev_a[:, b_ * TC:(b_ + 1) * TC]),
                        lambda e2: e2.dma_start(
                            out=bass.AP(sh_at, 1 * Hd * S + m_ * 128 * S + t_,
                                        [[S, 128], [1, TC]]),
                            in_=ev_a[:, b_ * TC:(b_ + 1) * TC]),
                        m_gp), waits=[evm], inc=16)
                    at_ev.append((m, evm, gpo))
                    if "atp" in taps and c == 0:
                        P.emit("gp", lambda e, m_=m, b_=eb: e.dma_start(
                            out=tap_t["atp"].ap()[:, m_ * TC:(m_ + 1) * TC],
                            in_=ev_a[:, b_ * TC:(b_ + 1) * TC]), waits=[evm], inc=16)
                        P.barrier()

                st_done = P.mark("st")
                st_done2 = P.mark("gp")
                P.emit("sy", lambda e, c_=c: _br2(
                    e,
                    lambda e2: e2.dma_start(out=bass.AP(sh_fl, 0, [[16, 1], [1, 1]]),
                                            in_=c_fv[0:1, c_:c_ + 1]),
                    lambda e2: e2.dma_start(out=bass.AP(sh_fl, 16, [[16, 1], [1, 1]]),
                                            in_=c_fv[0:1, c_:c_ + 1]),
                    m_sy), waits=[st_done, st_done2], inc=16)

            P.barrier()
            for t_, sh_ in (("shat", sh_at), ("shff", sh_ff)):
                if t_ in taps:
                    for m in range(NKT):
                        P.emit("gp", lambda e, tt=t_, ss=sh_, m_=m: _br2(
                            e,
                            lambda e2: e2.dma_start(
                                out=tap_t[tt].ap()[m_ * 128:(m_ + 1) * 128, :],
                                in_=bass.AP(ss, m_ * 128 * S, [[S, 128], [1, S]])),
                            lambda e2: e2.dma_start(
                                out=tap_t[tt].ap()[m_ * 128:(m_ + 1) * 128, :],
                                in_=bass.AP(ss, Hd * S + m_ * 128 * S, [[S, 128], [1, S]])),
                            m_gp), inc=16)
                    P.barrier()

            def poll(e):
                slot = _slotreg(e)
                expv = e.alloc_register("expr")
                e.reg_load(expv, c_fv[0:1, NCH - 1:NCH])
                fv = e.alloc_register("fvr")
                pc = e.alloc_register("pcr")
                cont = e.alloc_register("contr")
                e.reg_mov(fv, 0)
                e.reg_mov(pc, 0)
                e.reg_mov(cont, 1)
                with e.While(cont):
                    e.reg_add(pc, pc, 16)
                    with e.If_eq(slot, 0):
                        e.dma_start(out=fl_t[:1, :1],
                                    in_=bass.AP(sh_fl, 16, [[16, 1], [1, 1]])
                                    ).then_inc(m_poll, 16)
                    with e.Else():
                        e.dma_start(out=fl_t[:1, :1],
                                    in_=bass.AP(sh_fl, 0, [[16, 1], [1, 1]])
                                    ).then_inc(m_poll, 16)
                    e.wait_ge(m_poll, pc)
                    e.reg_load(fv, fl_t[:1, :1])
                    e.reg_alu(cont, fv, expv, ALU.not_equal)
                return None

            P.emit("gp", poll, inc=0)
            poll_m = P.mark("gp")

            out_prev = [None]
            f_am, f_ap = dAt[:, 0:TC], sg_t[:, 0:TC]
            f_fm, f_fp = dtsp[:, 0:TC], dtsp[:, TC:2 * TC]
            f_sq = stt_[:]
            f_o = ydt[:]
            f_rr = brep[:]
            f_x = xnT
            f_ss = dtx[0:1, 0:TC]
            f_rs = dtx[0:1, TC:2 * TC]

            for lc in range(2):
                P.barrier()
                xl_marks = []
                tp_ev = []
                for sub in range(NSUB):
                    def xload(e, s=sub, l=lc):
                        o0 = (l * TC + s * 128) * Hd
                        o1 = ((2 + l) * TC + s * 128) * Hd
                        return _br2(
                            e,
                            lambda e2: e2.dma_start(
                                out=xn1[:], in_=bass.AP(x_in, o0, [[Hd, 128], [1, Hd]])),
                            lambda e2: e2.dma_start(
                                out=xn1[:], in_=bass.AP(x_in, o1, [[Hd, 128], [1, Hd]])),
                            m_gp)
                    wts = [poll_m] + ([tp_ev[-1]] if tp_ev else [])
                    xl = P.emit("gp", xload, waits=wts, inc=16)
                    for kt in range(NKT):
                        pst = ps0 if (sub * NKT + kt) % 2 == 0 else ps1
                        src = bass.AP(xn1, kt * 128, [[Hd, 128], [1, 128]])
                        w = [xl]
                        if len(tp_ev) >= 2:
                            w.append(tp_ev[-2])
                        P.emit("pe", lambda e, p_=pst, s_=src: e.transpose(
                            p_[:, 0:128], s_, c_ident[:]), waits=w)
                        tpm = P.mark("pe")
                        P.emit("act", lambda e, p_=pst, k_=kt, s_=sub: e.activation(
                            out=f_x[:, k_ * TC + s_ * 128:k_ * TC + s_ * 128 + 128],
                            in_=p_[:, 0:128], func=AF.Identity), waits=[tpm])
                        tp_ev.append(P.mark("act"))
                fx_done = tp_ev[-1]
                if "xf" in taps and lc == 0:
                    P.emit("gp", lambda e: e.dma_start(out=tap_t["xf"].ap(), in_=f_x[:]),
                           waits=[fx_done], inc=16)
                    P.barrier()

                ffsum_writes = []
                sq_last = None
                for m in range(NKT):
                    P.barrier()
                    def rd_fm(e, m_=m, l=lc):
                        o0 = 0 * Hd * S + m_ * 128 * S + l * TC
                        o1 = 1 * Hd * S + m_ * 128 * S + (2 + l) * TC
                        return _br2(
                            e,
                            lambda e2: e2.dma_start(
                                out=f_fm, in_=bass.AP(sh_ff, o0, [[S, 128], [1, TC]])),
                            lambda e2: e2.dma_start(
                                out=f_fm, in_=bass.AP(sh_ff, o1, [[S, 128], [1, TC]])),
                            m_gp)

                    def rd_fp(e, m_=m, l=lc):
                        o0 = 1 * Hd * S + m_ * 128 * S + l * TC
                        o1 = 0 * Hd * S + m_ * 128 * S + (2 + l) * TC
                        return _br2(
                            e,
                            lambda e2: e2.dma_start(
                                out=f_fp, in_=bass.AP(sh_ff, o0, [[S, 128], [1, TC]])),
                            lambda e2: e2.dma_start(
                                out=f_fp, in_=bass.AP(sh_ff, o1, [[S, 128], [1, TC]])),
                            m_gp)

                    prev = P.mark("dve")
                    preva = P.mark("act")
                    w_rd = [prev, preva]
                    if ffsum_writes:
                        w_rd.append(ffsum_writes[-1])
                    r1 = P.emit("gp", rd_fm, waits=w_rd, inc=16)
                    r2 = P.emit("gp", rd_fp, inc=16)
                    P.emit("dve", lambda e: e.tensor_tensor(f_fm, f_fm, f_fp, ALU.add),
                           waits=[r1, r2])
                    fsum_m = P.mark("dve")
                    ffw = P.emit("gp", lambda e, m_=m, l=lc: e.dma_start(
                        out=ffsum.ap()[m_ * 128:(m_ + 1) * 128, l * TC:(l + 1) * TC],
                        in_=f_fm), waits=[fsum_m], inc=16)
                    ffsum_writes.append(ffw)
                    sq_w = [fsum_m]
                    if sq_last is not None:
                        sq_w.append(sq_last)
                    P.emit("act", lambda e: e.activation(out=f_sq, in_=f_fm,
                                                         func=AF.Square),
                           waits=sq_w)
                    sqm = P.mark("act")
                    P.emit("pe", lambda e, m_=m: e.matmul(
                        ps2[0:1, 0:TC], c_ones[:], f_sq,
                        start=(m_ == 0), stop=(m_ == NKT - 1)), waits=[sqm])
                    sq_last = P.mark("pe")
                P.emit("act", lambda e: e.activation(
                    out=f_rs, in_=ps2[0:1, 0:TC], func=AF.Sqrt,
                    scale=1.0 / Hd, bias=c_eps[0:1, 1:2]), waits=[sq_last])
                rs_m = P.mark("act")
                P.emit("dve", lambda e: e.reciprocal(f_ss, f_rs), waits=[rs_m])
                rr_w = P.mark("dve")
                P.emit("gp", lambda e: e.dma_start(out=bscr.ap()[0:1, 0:TC], in_=f_ss),
                       waits=[rr_w], inc=16)
                bb_m = P.mark("gp")
                P.emit("sy", lambda e: e.dma_start(
                    out=f_rr, in_=bass.AP(bscr, 0, [[0, 128], [1, TC]])),
                    waits=[bb_m], inc=16)
                rr_m = P.mark("sy")

                for m in range(NKT):
                    P.barrier()
                    def rd_am(e, m_=m, l=lc):
                        o0 = 0 * Hd * S + m_ * 128 * S + l * TC
                        o1 = 1 * Hd * S + m_ * 128 * S + (2 + l) * TC
                        return _br2(
                            e,
                            lambda e2: e2.dma_start(
                                out=f_am, in_=bass.AP(sh_at, o0, [[S, 128], [1, TC]])),
                            lambda e2: e2.dma_start(
                                out=f_am, in_=bass.AP(sh_at, o1, [[S, 128], [1, TC]])),
                            m_gp)

                    def rd_ap(e, m_=m, l=lc):
                        o0 = 1 * Hd * S + m_ * 128 * S + l * TC
                        o1 = 0 * Hd * S + m_ * 128 * S + (2 + l) * TC
                        return _br2(
                            e,
                            lambda e2: e2.dma_start(
                                out=f_ap, in_=bass.AP(sh_at, o0, [[S, 128], [1, TC]])),
                            lambda e2: e2.dma_start(
                                out=f_ap, in_=bass.AP(sh_at, o1, [[S, 128], [1, TC]])),
                            m_gp)

                    def rd_fs(e, m_=m, l=lc):
                        return e.dma_start(
                            out=f_fm,
                            in_=ffsum.ap()[m_ * 128:(m_ + 1) * 128, l * TC:(l + 1) * TC])

                    prev = P.mark("dve")
                    r1 = P.emit("gp", rd_am, waits=[prev], inc=16)
                    r2 = P.emit("gp", rd_ap, inc=16)
                    r3 = P.emit("gp", rd_fs, waits=[ffsum_writes[-1]], inc=16)
                    P.emit("dve", lambda e: e.tensor_tensor(f_am, f_am, f_ap, ALU.add),
                           waits=[r1, r2, rr_m, fx_done])
                    P.emit("dve", lambda e, m_=m: e.tensor_tensor(
                        f_am, f_am, f_x[:, m_ * TC:(m_ + 1) * TC], ALU.add))
                    P.emit("dve", lambda e: e.tensor_tensor(f_fp, f_fm, f_rr, ALU.mult),
                           waits=[r3])
                    P.emit("dve", lambda e: e.tensor_tensor(f_o, f_am, f_fp, ALU.add),
                           waits=([out_prev[0]] if out_prev[0] else []))
                    om = P.mark("dve")
                    out_prev[0] = P.emit("gp", lambda e, m_=m, l=lc: e.dma_start(
                        out=out.ap()[m_ * 128:(m_ + 1) * 128, l * TC:(l + 1) * TC],
                        in_=f_o), waits=[om], inc=16)

            last_wait = {}

            def emit_stream(eng_name, engine):
                for waits, fn, inc, sem in P.q[eng_name]:
                    for sem_name, val in waits:
                        key = (eng_name, sem_name)
                        if val > 0 and last_wait.get(key, -1) < val:
                            engine.wait_ge(SEM[sem_name], val)
                            last_wait[key] = val
                    inst = fn(engine)
                    if inc:
                        if inst == "incd":
                            pass
                        elif inst is not None:
                            inst.then_inc(SEM[sem], inc)
                        else:
                            engine.sem_inc(SEM[sem], inc)

            @block.tensor
            def _(t):
                emit_stream("pe", t)

            @block.vector
            def _(v):
                emit_stream("dve", v)

            @block.scalar
            def _(s):
                emit_stream("act", s)

            @block.sync
            def _(s):
                emit_stream("sy", s)

            @block.gpsimd
            def _(g):
                emit_stream("gp", g)
                g.wait_ge(m_gp, P.cnt["gp"])

    return nc


_GPR = {}


def _slotreg(e):
    key = id(e)
    if key not in _GPR:
        pid = e.partition_id()
        slot = e.alloc_register("slotr")
        e.reg_alu(slot, pid, 1, ALU.bitwise_and)
        _GPR[key] = slot
    return _GPR[key]


def _br2(e, fn0, fn1, incsem=None):
    slot = _slotreg(e)
    with e.If_eq(slot, 0):
        i0 = fn0(e)
        if incsem is not None and i0 is not None:
            i0.then_inc(incsem, 16)
    with e.Else():
        i1 = fn1(e)
        if incsem is not None and i1 is not None:
            i1.then_inc(incsem, 16)
    return "incd"


def kernel(**inp):
    global _LAST
    x_full = np.asarray(inp["L0_attn_hidden_states"])
    W1 = np.asarray(inp["L0_attn_in_proj_w"])
    cw = np.asarray(inp["L0_attn_conv_w"])
    cb = np.asarray(inp["L0_attn_conv_b"])
    xpw = np.asarray(inp["L0_attn_x_proj_weight"])
    dtw = np.asarray(inp["L0_attn_dt_proj_weight"])
    dtb = np.asarray(inp["L0_attn_dt_proj_bias"])
    alog = np.asarray(inp["L0_attn_A_log"])
    Dv = np.asarray(inp["L0_attn_D"])
    Wo = np.asarray(inp["L0_attn_out_proj_w"])
    gup = np.asarray(inp["L0_ffn_gate_up_packed"])
    gus = np.asarray(inp["L0_ffn_gate_up_scales"])
    dnp = np.asarray(inp["L0_ffn_down_packed"])
    dns = np.asarray(inp["L0_ffn_down_scales"])

    hperm = np.array([256 * (kt >> 1) + 2 * p + (kt & 1)
                      for kt in range(NKT) for p in range(128)])
    nonce = int.from_bytes(os.urandom(3), "little") + 7
    fv_nonce = (nonce + np.arange(1, 9, dtype=np.int64)).astype(np.int32).reshape(1, 8)
    in_maps = []
    for c in range(8):
        b, hh = c >> 1, c & 1
        isl = slice(hh * HD, (hh + 1) * HD)
        rows = 2 * np.arange(hh * HD, (hh + 1) * HD)
        in_maps.append(dict(
            x_in=np.ascontiguousarray(x_full[b]),
            w1x=np.ascontiguousarray(
                W1[rows, :].T[hperm].reshape(NKT, 128, NKT, 128)
                .transpose(1, 2, 0, 3).reshape(128, -1)),
            w1g=np.ascontiguousarray(
                W1[rows + 1, :].T[hperm].reshape(NKT, 128, NKT, 128)
                .transpose(1, 2, 0, 3).reshape(128, -1)),
            xw=np.ascontiguousarray(xpw[hh].T),
            dtw=np.ascontiguousarray(dtw[hh].T),
            dtbias=np.ascontiguousarray(dtb[hh].reshape(ND, 128).T),
            alog=np.ascontiguousarray(
                alog[hh].reshape(ND, 128, NN).transpose(1, 0, 2).reshape(128, ND * NN)),
            dvec=np.ascontiguousarray(Dv[hh].reshape(ND, 128).T),
            convw=np.ascontiguousarray(
                cw[isl].reshape(ND, 128, 4).transpose(1, 0, 2).reshape(128, ND * 4)),
            convb=np.ascontiguousarray(cb[isl].reshape(ND, 128).T),
            wo=np.ascontiguousarray(
                Wo[:, isl].T.reshape(ND, 128, NKT, 128)
                .transpose(1, 2, 0, 3).reshape(128, -1)),
            gpk=np.ascontiguousarray(
                gup[hh * OG:(hh + 1) * OG].view(np.uint8).reshape(OG, Hd // 2).T),
            upk=np.ascontiguousarray(
                gup[2 * OG + hh * OG:2 * OG + (hh + 1) * OG].view(np.uint8)
                .reshape(OG, Hd // 2).T),
            dpk=np.ascontiguousarray(
                dnp.view(np.uint8).reshape(Hd, OG)[:, hh * (OG // 2):(hh + 1) * (OG // 2)].T),
            gsc=np.ascontiguousarray(gus[:, hh * OG:(hh + 1) * OG]),
            usc=np.ascontiguousarray(gus[:, 2 * OG + hh * OG:2 * OG + (hh + 1) * OG]),
            dsc=np.ascontiguousarray(dns[hh * 22:(hh + 1) * 22]),
            fvals=fv_nonce,
        ))

    nc = build_nc(taps=_TAPS)
    res = run_bass_kernel_spmd(nc, in_maps, core_ids=list(range(8)))
    _LAST = {c: res.results[c] for c in range(8)}

    outp = np.empty((4, S, Hd), dtype=np.float32)
    for c in range(8):
        b, hh = c >> 1, c & 1
        outp[b, hh * (S // 2):(hh + 1) * (S // 2), :] = res.results[c]["out"].T
    return outp


# version 15
# speedup vs baseline: 1.0513x; 1.0030x over previous
import contextlib
import os

import numpy as np
import concourse.bass as bass
import concourse.mybir as mybir
from concourse.bass_utils import run_bass_kernel_spmd

F32 = mybir.dt.float32
BF16 = mybir.dt.bfloat16
F16 = mybir.dt.float16
U8 = mybir.dt.uint8
U16 = mybir.dt.uint16
I32 = mybir.dt.int32
AF = mybir.ActivationFunctionType
ALU = mybir.AluOpType
AXX = mybir.AxisListType.X

S = 2048
Hd = 2048
HD = 2048
TC = 512
NCH = S // TC
ND = HD // 128
NKT = Hd // 128
NN = 16
RR = 128
OG = 2816
NM2 = OG // 128
NSUB = TC // 128

ENGS = ("pe", "dve", "act", "sy", "gp")
XSEMS = ("b0", "b1", "c0", "c1", "wg0", "wg1", "gu", "dn0", "dn1",
         "ow0", "ow1", "xl", "st", "dq")

_TAPS = ()
_LAST = {}


class Prog:
    def __init__(self):
        self.q = {e: [] for e in ENGS}
        self.cnt = {e: 0 for e in ENGS}
        for s_ in XSEMS:
            self.cnt[s_] = 0
        self.pending = {e: [] for e in ENGS}

    def emit(self, eng, fn, waits=(), inc=1, sem=None):
        sem = sem or eng
        w = tuple(waits) + tuple(self.pending[eng])
        w = tuple((s, self.cnt[s]) if s not in ("pe", "dve", "act")
                  and self.cnt[s] > v else (s, v) for (s, v) in w)
        self.pending[eng] = []
        self.q[eng].append((w, fn, inc, sem))
        self.cnt[sem] += inc
        return (sem, self.cnt[sem])

    def mark(self, eng):
        return (eng, self.cnt[eng])

    def barrier(self):
        marks = {e: self.mark(e) for e in (*ENGS, *XSEMS)}
        for e in ENGS:
            self.pending[e].extend(m for k, m in marks.items() if m[1] > 0)


def build_nc(taps=()):
    nc = bass.Bass("TRN2", target_bir_lowering=False)
    taps = set(taps)
    _GPR.clear()

    x_in = nc.dram_tensor("x_in", [S, Hd], F32, kind="ExternalInput")
    w1x = nc.dram_tensor("w1x", [128, NKT * NKT * 128], F32, kind="ExternalInput")
    w1g = nc.dram_tensor("w1g", [128, NKT * NKT * 128], F32, kind="ExternalInput")
    xw = nc.dram_tensor("xw", [HD, 160], F32, kind="ExternalInput")
    dtw = nc.dram_tensor("dtw", [RR, HD], F32, kind="ExternalInput")
    dtbias = nc.dram_tensor("dtbias", [128, ND], F32, kind="ExternalInput")
    alog = nc.dram_tensor("alog", [128, ND * NN], F32, kind="ExternalInput")
    dvec = nc.dram_tensor("dvec", [128, ND], F32, kind="ExternalInput")
    convw = nc.dram_tensor("convw", [128, ND * 4], F32, kind="ExternalInput")
    convb = nc.dram_tensor("convb", [128, ND], F32, kind="ExternalInput")
    wo = nc.dram_tensor("wo", [128, ND * NKT * 128], F32, kind="ExternalInput")
    gpk = nc.dram_tensor("gpk", [Hd // 2, OG], U8, kind="ExternalInput")
    upk = nc.dram_tensor("upk", [Hd // 2, OG], U8, kind="ExternalInput")
    dpk = nc.dram_tensor("dpk", [OG // 2, Hd], U8, kind="ExternalInput")
    gsc = nc.dram_tensor("gsc", [16, OG], F16, kind="ExternalInput")
    usc = nc.dram_tensor("usc", [16, OG], F16, kind="ExternalInput")
    dsc = nc.dram_tensor("dsc", [22, Hd], F16, kind="ExternalInput")
    fvals = nc.dram_tensor("fvals", [1, 8], I32, kind="ExternalInput")
    out = nc.dram_tensor("out", [Hd, S // 2], F32, kind="ExternalOutput")

    ident_d = nc.inline_tensor(np.eye(128, dtype=np.float32), "identc")

    wg_f = nc.dram_tensor("wg_f", [128, NM2 * NKT * 128], F32)
    wu_f = nc.dram_tensor("wu_f", [128, NM2 * NKT * 128], F32)
    wd_f = nc.dram_tensor("wd_f", [128, NKT * NM2 * 128], F32)
    bscr = nc.dram_tensor("bscr", [NN, S], F32)
    cscr = nc.dram_tensor("cscr", [NN, S], F32)
    ffsum = nc.dram_tensor("ffsum", [Hd, 2 * TC], F32)
    ygs = nc.dram_tensor("ygs", [128, ND * TC], F32)
    sh_at = nc.dram_tensor("sh_at", [2, Hd, S], F32, addr_space="Shared")
    sh_ff = nc.dram_tensor("sh_ff", [2, Hd, S], F32, addr_space="Shared")
    sh_fl = nc.dram_tensor("shflag", [2, 16], I32, addr_space="Shared")

    tap_shapes = {
        "xn": [TC, Hd], "xnT": [128, NKT * TC], "xs": [128, ND * TC],
        "dtin": [RR, TC], "bmat": [16, TC], "cmat": [16, TC],
        "yg": [128, ND * TC], "hffn": [128, NM2 * TC],
        "atp": [128, NKT * TC], "ffp": [128, NKT * TC],
        "wg0": [128, OG], "wu0": [128, OG], "wd0": [128, Hd], "xf": [128, NKT * TC],
        "wgF": [Hd, OG], "wuF": [Hd, OG],
        "shat": [Hd, S], "shff": [Hd, S],
    }
    tap_t = {t: nc.dram_tensor("tap_" + t, tap_shapes[t], F32, kind="ExternalOutput")
             for t in taps}

    P = Prog()

    with contextlib.ExitStack() as _es:
        c_ident = _es.enter_context(nc.sbuf_tensor("c_ident", [128, 128], F32))
        c_dtb = _es.enter_context(nc.sbuf_tensor("c_dtb", [128, ND], F32))
        c_A = _es.enter_context(nc.sbuf_tensor("c_A", [128, ND * NN], F32))
        c_dv = _es.enter_context(nc.sbuf_tensor("c_dv", [128, ND], F32))
        c_cw = _es.enter_context(nc.sbuf_tensor("c_cw", [128, ND * 4], F32))
        c_cb = _es.enter_context(nc.sbuf_tensor("c_cb", [128, ND], F32))
        c_dtw = _es.enter_context(nc.sbuf_tensor("c_dtw", [128, HD], F32))
        c_ones = _es.enter_context(nc.sbuf_tensor("c_ones", [128, 1], F32))
        c_eps = _es.enter_context(nc.sbuf_tensor("c_eps", [128, 2], F32))
        c_fv = _es.enter_context(nc.sbuf_tensor("c_fv", [1, 8], I32))
        carry = _es.enter_context(nc.sbuf_tensor("carry", [128, ND * NN], F32))
        halo = _es.enter_context(nc.sbuf_tensor("halo", [128, ND * 4], F32))
        fl_t = _es.enter_context(nc.sbuf_tensor("fl_t", [1, 1], I32))
        xn1 = _es.enter_context(nc.sbuf_tensor("xn1", [128, Hd], F32))
        ln_s = _es.enter_context(nc.sbuf_tensor("ln_s", [128, 4], F32))
        xnT = _es.enter_context(nc.sbuf_tensor("xnT", [128, NKT * TC], F32))
        xs_t = _es.enter_context(nc.sbuf_tensor("xs_t", [128, ND * TC], F32))
        hff = _es.enter_context(nc.sbuf_tensor("hff", [128, NM2 * TC], F32))
        w1b = _es.enter_context(nc.sbuf_tensor("w1b", [128, 2 * NKT * 128], F32))
        war = _es.enter_context(nc.sbuf_tensor("war", [128, 4 * NKT * 128], F32))
        ps0 = _es.enter_context(nc.psum_tensor("ps0", [128, TC], F32))
        ps1 = _es.enter_context(nc.psum_tensor("ps1", [128, TC], F32))
        ps2 = _es.enter_context(nc.psum_tensor("ps2", [128, TC], F32))
        ps3 = _es.enter_context(nc.psum_tensor("ps3", [128, TC], F32))
        ps4 = _es.enter_context(nc.psum_tensor("ps4", [128, TC], F32))
        ps5 = _es.enter_context(nc.psum_tensor("ps5", [128, TC], F32))
        ps6 = _es.enter_context(nc.psum_tensor("ps6", [128, TC], F32))
        ps7 = _es.enter_context(nc.psum_tensor("ps7", [128, TC], F32))
        m_pe = _es.enter_context(nc.semaphore("m_pe"))
        m_dve = _es.enter_context(nc.semaphore("m_dve"))
        m_act = _es.enter_context(nc.semaphore("m_act"))
        m_sy = _es.enter_context(nc.semaphore("m_sy"))
        m_gp = _es.enter_context(nc.semaphore("m_gp"))
        m_poll = _es.enter_context(nc.semaphore("m_poll"))
        m_xs = {s_: _es.enter_context(nc.semaphore("m_" + s_)) for s_ in XSEMS}
        block = _es.enter_context(nc.Block())
        SEM = {"pe": m_pe, "dve": m_dve, "act": m_act, "sy": m_sy, "gp": m_gp}
        SEM.update(m_xs)

        c_A2 = xnT[:, 0:ND * NN]

        def ktoff(kt):
            return 256 * (kt >> 1) + (kt & 1)

        def ld(eng, dst, src):
            return P.emit(eng, lambda e, d=dst, s=src: e.dma_start(out=d, in_=s),
                          inc=16)

        ld("gp", c_ident[:], ident_d.ap())
        ld("gp", c_dtb[:], dtbias.ap())
        ld("gp", c_A2[:], alog.ap())
        ld("gp", c_dv[:], dvec.ap())
        ld("gp", c_cw[:], convw.ap())
        ld("gp", c_cb[:], convb.ap())
        ld("gp", c_dtw[:], dtw.ap())
        ld("gp", c_fv[:], fvals.ap())
        k0 = P.mark("gp")
        P.emit("act", lambda e: e.activation(out=c_A[:], in_=c_A2[:], func=AF.Exp),
               waits=[k0])
        aexp = P.mark("act")
        P.emit("dve", lambda e: e.tensor_scalar(c_A[:], c_A[:], -1.0, None, ALU.mult),
               waits=[aexp])
        P.emit("dve", lambda e: e.memset(carry[:], 0.0))
        P.emit("dve", lambda e: e.memset(halo[:], 0.0))
        P.emit("dve", lambda e: e.memset(c_ones[:], 1.0))
        P.emit("dve", lambda e: e.memset(c_eps[:, 0:1], 1e-5))
        P.emit("dve", lambda e: e.memset(c_eps[:, 1:2], 1e-6))

        OH = 768

        _dq = contextlib.ExitStack()
        q_b8 = _dq.enter_context(nc.sbuf_tensor("q_b8", [128, OH], U8))
        q_w16 = _dq.enter_context(nc.sbuf_tensor("q_w16", [128, OH], U16))
        q_nib = _dq.enter_context(nc.sbuf_tensor("q_nib", [128, OH], U16))
        q_t1 = _dq.enter_context(nc.sbuf_tensor("q_t1", [128, OH], U16))
        q_t2 = _dq.enter_context(nc.sbuf_tensor("q_t2", [128, OH], U16))
        q_v1 = _dq.enter_context(nc.sbuf_tensor("q_v1", [128, OH], BF16))
        q_s16 = _dq.enter_context(nc.sbuf_tensor("q_s16", [128, OH], F16))
        q_s32 = _dq.enter_context(nc.sbuf_tensor("q_s32", [128, OH], F32))
        q_p = _dq.enter_context(nc.sbuf_tensor("q_p", [128, 2 * OH], F32))
        if True:
            s32_prev = [None]

            def dequant(src_pk, src_sc, n_blk, ocols, dst, perm_cols, mstride):
                if perm_cols:
                    ohs = [(0, 768), (768, 768), (1536, 768), (2304, 512)]
                else:
                    ohs = [(0, 768), (768, 768), (1536, 512)]
                for b_ in range(n_blk):
                    for (obase, oc) in ohs:
                        l0 = ld("sy", q_b8[:, :oc],
                                src_pk.ap()[b_ * 128:(b_ + 1) * 128, obase:obase + oc])
                        src_s = bass.AP(src_sc, 2 * b_ * ocols + obase,
                                        [[ocols, 2], [0, 64], [1, oc]])
                        l1 = P.emit("sy", lambda e, s=src_s, c_=oc: e.dma_start(
                            out=q_s16[:, :c_], in_=s),
                            waits=([s32_prev[0]] if s32_prev[0] else []), inc=16)
                        P.emit("dve", lambda e, c_=oc: e.tensor_copy(
                            q_w16[:, :c_], q_b8[:, :c_]), waits=[l0])
                        P.emit("dve", lambda e, c_=oc: e.tensor_copy(
                            q_s32[:, :c_], q_s16[:, :c_]), waits=[l1])
                        s32_prev[0] = P.mark("dve")
                        for par in range(2):
                            if par == 0:
                                P.emit("dve", lambda e, c_=oc: e.tensor_scalar(
                                    q_nib[:, :c_], q_w16[:, :c_], 15, None,
                                    ALU.bitwise_and))
                            else:
                                P.emit("dve", lambda e, c_=oc: e.tensor_scalar(
                                    q_nib[:, :c_], q_w16[:, :c_], 4, 15,
                                    ALU.logical_shift_right, ALU.bitwise_and))
                            P.emit("dve", lambda e, c_=oc: e.tensor_scalar(
                                q_t1[:, :c_], q_nib[:, :c_], 7, 6,
                                ALU.bitwise_and, ALU.logical_shift_left))
                            P.emit("dve", lambda e, c_=oc: e.tensor_scalar(
                                q_t2[:, :c_], q_nib[:, :c_], 8, 12,
                                ALU.bitwise_and, ALU.logical_shift_left))
                            P.emit("dve", lambda e, c_=oc: e.tensor_tensor(
                                q_t1[:, :c_], q_t1[:, :c_], q_t2[:, :c_],
                                ALU.bitwise_or))
                            P.emit("dve", lambda e, c_=oc: e.tensor_scalar(
                                q_v1[:, :c_], q_t1[:, :c_].bitcast(BF16),
                                float(2.0 ** 63), float(2.0 ** 63),
                                ALU.mult, ALU.mult))
                            if perm_cols:
                                nb = oc // 256
                                po = bass.AP(q_p, par * OH,
                                             [[2 * OH, 128], [256, nb], [1, 128], [128, 2]])
                                pv = bass.AP(q_v1, 0,
                                             [[OH, 128], [256, nb], [2, 128], [1, 2]])
                                psc = bass.AP(q_s32, 0,
                                              [[OH, 128], [256, nb], [2, 128], [1, 2]])
                                P.emit("dve", lambda e, a=po, v=pv, s=psc:
                                       e.tensor_tensor(a, v, s, ALU.mult))
                            else:
                                P.emit("dve", lambda e, p_=par, c_=oc: e.tensor_tensor(
                                    q_p[:, p_ * OH:p_ * OH + c_], q_v1[:, :c_],
                                    q_s32[:, :c_], ALU.mult))
                        dv = P.mark("dve")
                        rl = n_blk * 2 * 128 * mstride // 128 if False else None
                        for par in range(2):
                            kt2 = None
                            P.emit("sy", lambda e, b2=b_, p2=par, ob=obase, c_=oc,
                                   d2=dst, ms=mstride: e.dma_start(
                                       out=bass.AP(
                                           d2, (ob // 128) * ms + (2 * b2 + p2) * 128,
                                           [[NM2 * NKT * 128, 128],
                                            [ms, c_ // 128], [1, 128]]),
                                       in_=q_p[:, p2 * OH:p2 * OH + c_]),
                                   waits=[dv], inc=16, sem="dq")

            dequant(gpk, gsc, 8, OG, wg_f, True, NKT * 128)
            dequant(upk, usc, 8, OG, wu_f, True, NKT * 128)
            dequant(dpk, dsc, 11, Hd, wd_f, False, NM2 * 128)
        _dq.close()
        deq_gp = P.mark("dq")
        deq_dve = P.mark("dve")

        for t, d2 in (("wg0", wg_f), ("wu0", wu_f), ("wd0", wd_f)):
            if t in taps:
                P.emit("gp", lambda e, t_=t, s=d2: e.dma_start(
                    out=tap_t[t_].ap(), in_=s.ap()[0:128, :]),
                    waits=[deq_gp], inc=16)
        for t, d2 in (("wgF", wg_f), ("wuF", wu_f)):
            if t in taps:
                P.emit("gp", lambda e, t_=t, s=d2: e.dma_start(
                    out=tap_t[t_].ap(), in_=s.ap()),
                    waits=[deq_gp], inc=16)
        if taps:
            P.barrier()

        xpj = _es.enter_context(nc.sbuf_tensor("xpj", [128, TC + 3], F32))
        dtin = _es.enter_context(nc.sbuf_tensor("dtin", [128, TC], F32))
        dtsp = _es.enter_context(nc.sbuf_tensor("dtsp", [128, 2 * TC], F32))
        dtx = _es.enter_context(nc.sbuf_tensor("dtx", [128, 2 * TC], F32))
        brep = _es.enter_context(nc.sbuf_tensor("brep", [128, TC], F32))
        crep = _es.enter_context(nc.sbuf_tensor("crep", [128, TC], F32))
        dAt = _es.enter_context(nc.sbuf_tensor("dAt", [128, TC], F32))
        dBu = _es.enter_context(nc.sbuf_tensor("dBu", [128, TC], F32))
        stt_ = _es.enter_context(nc.sbuf_tensor("stt_", [128, TC], F32))
        tmp_ = _es.enter_context(nc.sbuf_tensor("tmp_", [128, TC], F32))
        ydt = _es.enter_context(nc.sbuf_tensor("ydt", [128, TC], F32))
        sg_t = _es.enter_context(nc.sbuf_tensor("sg_t", [128, TC], F32))
        ev_a = _es.enter_context(nc.sbuf_tensor("ev_a", [128, 2 * TC], F32))
        if True:
            c_xw = war[:, 5632:5632 + ND * 160]
            bc_s = sg_t
            wgb = war[:, 0:NKT * 128]
            wub = war[:, NKT * 128:2 * NKT * 128]
            wdb = war[:, 0:2 * NM2 * 128]
            wob = war[:, 0:2 * NKT * 128]
            z_sp = tmp_
            for c in range(NCH):
                t0 = c * TC
                P.barrier()

                ev_done = []
                for sub in range(NSUB):
                    lx = P.emit("gp", lambda e, s=sub, t_=t0: e.dma_start(
                        out=xn1[:], in_=x_in.ap()[t_ + s * 128:t_ + (s + 1) * 128, :]),
                        waits=([ev_done[-1]] if ev_done else []), inc=16, sem="xl")
                    P.emit("dve", lambda e: e.reduce_sum(ln_s[:, 0:1], xn1[:], axis=AXX),
                           waits=[lx])
                    rdc_m = P.mark("dve")
                    P.emit("dve", lambda e: e.tensor_scalar(
                        ln_s[:, 1:2], ln_s[:, 0:1], 1.0 / Hd, None, ALU.mult),
                        waits=[rdc_m])
                    mu_m = P.mark("dve")
                    P.emit("dve", lambda e: e.tensor_scalar(
                        xn1[:], xn1[:], ln_s[:, 1:2], None, ALU.subtract),
                        waits=[mu_m])
                    xc_m = P.mark("dve")
                    P.emit("act", lambda e: e.activation(
                        out=xs_t[:, 0:Hd], in_=xn1[:],
                        func=AF.Square, accum_out=ln_s[:, 2:3]), waits=[xc_m])
                    sq_m = P.mark("act")
                    P.emit("act", lambda e: e.activation(
                        out=ln_s[:, 3:4], in_=ln_s[:, 2:3], func=AF.Sqrt,
                        scale=1.0 / Hd, bias=c_eps[:, 0:1]), waits=[sq_m])
                    sd_m = P.mark("act")
                    P.emit("dve", lambda e: e.reciprocal(ln_s[:, 2:3], ln_s[:, 3:4]),
                           waits=[sd_m])
                    rs_self = P.mark("dve")
                    P.emit("dve", lambda e: e.tensor_scalar(
                        xn1[:], xn1[:], ln_s[:, 2:3], None, ALU.mult),
                        waits=[rs_self])
                    xn_m = P.mark("dve")
                    if "xn" in taps and c == 0:
                        P.emit("gp", lambda e, s=sub: e.dma_start(
                            out=tap_t["xn"].ap()[s * 128:(s + 1) * 128, :], in_=xn1[:]),
                            waits=[xn_m], inc=16)
                        P.barrier()
                    for kt in range(NKT):
                        pst = ps0 if (sub * NKT + kt) % 2 == 0 else ps1
                        src = bass.AP(xn1, ktoff(kt), [[Hd, 128], [2, 128]])
                        w = [xn_m]
                        if len(ev_done) >= 2:
                            w.append(ev_done[-2])
                        P.emit("pe", lambda e, p_=pst, s_=src: e.transpose(
                            p_[:, 0:128], s_, c_ident[:]), waits=w)
                        tp_m = P.mark("pe")
                        P.emit("act", lambda e, p_=pst, k_=kt, s_=sub: e.activation(
                            out=xnT[:, k_ * TC + s_ * 128:k_ * TC + s_ * 128 + 128],
                            in_=p_[:, 0:128], func=AF.Identity), waits=[tp_m])
                        ev_done.append(P.mark("act"))
                xnT_done = ev_done[-1]
                if "xnT" in taps and c == 0:
                    P.emit("gp", lambda e: e.dma_start(out=tap_t["xnT"].ap(), in_=xnT[:]),
                           waits=[xnT_done], inc=16)
                    P.barrier()

                for k in range(ND):
                    P.emit("sy", lambda e, k_=k: e.dma_start(
                        out=c_xw[:, k_ * 160:(k_ + 1) * 160],
                        in_=xw.ap()[k_ * 128:(k_ + 1) * 128, :]), inc=16, sem="gu")
                cxw_m = P.mark("gu")
                conv_marks = []
                for k in range(ND):
                    buf = k % 2
                    wsl = slice(buf * NKT * 128, (buf + 1) * NKT * 128)
                    lw = P.emit("gp", lambda e, k_=k, ws=wsl: e.dma_start(
                        out=w1b[:, ws],
                        in_=bass.AP(w1x, k_ * NKT * 128,
                                    [[NKT * NKT * 128, 128], [1, NKT * 128]])),
                        waits=([conv_marks[k - 2][2]] if k >= 2 else []), inc=16,
                        sem="wg0" if k % 2 == 0 else "wg1")
                    psD = ps2 if k % 2 == 0 else ps3
                    w_mm = [lw, xnT_done]
                    if k >= 2:
                        w_mm.append(conv_marks[k - 2][1])
                    for kt in range(NKT):
                        P.emit("pe", lambda e, k2=kt, ws=wsl, p_=psD: e.matmul(
                            p_[:], w1b[:, ws.start + k2 * 128:ws.start + (k2 + 1) * 128],
                            xnT[:, k2 * TC:(k2 + 1) * TC],
                            start=(k2 == 0), stop=(k2 == NKT - 1)),
                            waits=(w_mm if kt == 0 else []))
                    mm_m = P.mark("pe")
                    xo = 0
                    wev = [mm_m]
                    if c == 0 and k == 0:
                        wev += [deq_dve, deq_gp]
                    P.emit("dve", lambda e, p_=psD, o=xo: e.tensor_copy(
                        xpj[:, o + 3:o + 3 + TC], p_[:]), waits=wev)
                    P.emit("dve", lambda e, k_=k, o=xo: e.tensor_copy(
                        xpj[:, o:o + 3], halo[:, k_ * 4:k_ * 4 + 3]))
                    P.emit("dve", lambda e, k_=k, o=xo: e.tensor_copy(
                        halo[:, k_ * 4:k_ * 4 + 3], xpj[:, o + TC:o + TC + 3]))
                    hlo_m = P.mark("dve")
                    P.emit("dve", lambda e, k_=k, o=xo: e.tensor_scalar(
                        tmp_[:, 0:TC], xpj[:, o:o + TC],
                        c_cw[:, k_ * 4:k_ * 4 + 1], None, ALU.mult),
                        waits=[hlo_m])
                    for kk in range(1, 4):
                        P.emit("dve", lambda e, k_=k, o=xo, q=kk: e.scalar_tensor_tensor(
                            tmp_[:, 0:TC], xpj[:, o + q:o + q + TC],
                            c_cw[:, k_ * 4 + q:k_ * 4 + q + 1], tmp_[:, 0:TC],
                            ALU.mult, ALU.add))
                    cm = P.mark("dve")
                    P.emit("act", lambda e, k_=k: e.activation(
                        out=xs_t[:, k_ * TC:(k_ + 1) * TC], in_=tmp_[:, 0:TC],
                        func=AF.Silu, bias=c_cb[:, k_:k_ + 1]), waits=[cm])
                    xs_m = P.mark("act")
                    conv_marks.append((k, xs_m, P.mark("pe")))
                for k in range(ND):
                    w_xp = [conv_marks[k][1]] + ([cxw_m] if k == 0 else [])
                    P.emit("pe", lambda e, k_=k: e.matmul(
                        ps0[:, 0:TC], c_xw[:, k_ * 160:k_ * 160 + 128],
                        xs_t[:, k_ * TC:(k_ + 1) * TC],
                        start=(k_ == 0), stop=(k_ == ND - 1)), waits=w_xp)
                    P.emit("pe", lambda e, k_=k: e.matmul(
                        ps1[0:32, 0:TC], c_xw[:, k_ * 160 + 128:(k_ + 1) * 160],
                        xs_t[:, k_ * TC:(k_ + 1) * TC],
                        start=(k_ == 0), stop=(k_ == ND - 1)))
                xp_done = P.mark("pe")
                if "xs" in taps and c == 0:
                    P.emit("gp", lambda e: e.dma_start(out=tap_t["xs"].ap(), in_=xs_t[:]),
                           waits=[conv_marks[-1][1]], inc=16)
                    P.barrier()

                P.emit("dve", lambda e: e.tensor_copy(dtin[:], ps0[:, 0:TC]),
                       waits=[xp_done])
                P.emit("dve", lambda e: e.tensor_copy(bc_s[0:32, 0:TC], ps1[0:32, 0:TC]))
                bc_m = P.mark("dve")
                P.emit("gp", lambda e, t_=t0: e.dma_start(
                    out=bscr.ap()[:, t_:t_ + TC], in_=bc_s[0:16, 0:TC]),
                    waits=[bc_m], inc=16)
                P.emit("gp", lambda e, t_=t0: e.dma_start(
                    out=cscr.ap()[:, t_:t_ + TC], in_=bc_s[16:32, 0:TC]),
                    waits=[bc_m], inc=16)
                bc_hbm = P.mark("gp")
                for t_, sl_ in (("dtin", None), ("bmat", (0, 16)), ("cmat", (16, 32))):
                    if t_ in taps and c == 0:
                        srcap = dtin[:] if sl_ is None else bc_s[sl_[0]:sl_[1], 0:TC]
                        P.emit("gp", lambda e, a=tap_t[t_], s=srcap: e.dma_start(
                            out=a.ap(), in_=s), waits=[bc_m], inc=16)
                        P.barrier()

                dti_m = P.mark("dve")
                h_marks = []
                dn_ev = []
                gu_mm = {}
                gu_g = {}
                dn_mm = {}

                def feed_gu_mm(m):
                    wwait = [gu_g[m - 1]] if m >= 1 else [dti_m]
                    lg = P.emit("sy", lambda e, m_=m: e.dma_start(
                        out=wgb[:],
                        in_=bass.AP(wg_f, m_ * NKT * 128,
                                    [[NM2 * NKT * 128, 128], [1, NKT * 128]])),
                        waits=wwait + [deq_gp], inc=16, sem="gu")
                    ub = m % 2
                    usl = slice((1 + ub) * NKT * 128, (2 + ub) * NKT * 128)
                    uw_ = [gu_mm[m - 2][0]] if m >= 2 else [dti_m]
                    lu = P.emit("sy", lambda e, m_=m, us=usl: e.dma_start(
                        out=war[:, us],
                        in_=bass.AP(wu_f, m_ * NKT * 128,
                                    [[NM2 * NKT * 128, 128], [1, NKT * 128]])),
                        waits=uw_, inc=16, sem="dn0" if ub == 0 else "dn1")
                    psA, psB = (ps2, ps3) if m % 2 == 0 else (ps4, ps5)
                    w_mm = [lg, xnT_done]
                    w_mm.append(h_marks[m - 2][1] if m >= 2 else dti_m)
                    for kt in range(NKT):
                        P.emit("pe", lambda e, k_=kt, pA=psA: e.matmul(
                            pA[:], wgb[:, k_ * 128:(k_ + 1) * 128],
                            xnT[:, k_ * TC:(k_ + 1) * TC],
                            start=(k_ == 0), stop=(k_ == NKT - 1)),
                            waits=(w_mm if kt == 0 else []))
                    gu_g[m] = P.mark("pe")
                    for kt in range(NKT):
                        P.emit("pe", lambda e, k_=kt, pB=psB, us=usl: e.matmul(
                            pB[:], war[:, us.start + k_ * 128:us.start + (k_ + 1) * 128],
                            xnT[:, k_ * TC:(k_ + 1) * TC],
                            start=(k_ == 0), stop=(k_ == NKT - 1)),
                            waits=([lu] if kt == 0 else []))
                    gu_mm[m] = (P.mark("pe"), psA, psB)

                def feed_gu_post(m):
                    mm_m, psA, psB = gu_mm[m]
                    sw = [mm_m] + ([h_marks[m - 1][1]] if m >= 1 else [])
                    P.emit("act", lambda e, pA=psA: e.activation(
                        out=xpj[:, 0:TC], in_=pA[:], func=AF.Silu), waits=sw)
                    sg_m = P.mark("act")
                    P.emit("dve", lambda e, m_=m, pB=psB: e.tensor_tensor(
                        hff[:, m_ * TC:(m_ + 1) * TC], xpj[:, 0:TC],
                        pB[:], ALU.mult), waits=[sg_m])
                    h_marks.append((m, P.mark("dve")))

                DNB = [ps7, ps2, ps3, ps4, ps5]

                def feed_dn_mm(m):
                    buf = m % 2
                    wsl = slice(buf * NM2 * 128, (buf + 1) * NM2 * 128)
                    lw = P.emit("sy", lambda e, m_=m, ws=wsl: e.dma_start(
                        out=wdb[:, ws],
                        in_=bass.AP(wd_f, m_ * NM2 * 128,
                                    [[NKT * NM2 * 128, 128], [1, NM2 * 128]])),
                        waits=([dn_mm[m - 2][0]] if m >= 2 else [h_marks[-1][1]]),
                        inc=16, sem="dn0" if m % 2 == 0 else "dn1")
                    psC = DNB[m % 5]
                    w_mm = [lw, h_marks[-1][1]]
                    if m >= 5:
                        w_mm.append(dn_ev[m - 5][1])
                    for kt in range(NM2):
                        P.emit("pe", lambda e, k_=kt, ws=wsl, p_=psC: e.matmul(
                            p_[:], wdb[:, ws.start + k_ * 128:ws.start + (k_ + 1) * 128],
                            hff[:, k_ * TC:(k_ + 1) * TC],
                            start=(k_ == 0), stop=(k_ == NM2 - 1)),
                            waits=(w_mm if kt == 0 else []))
                    dn_mm[m] = (P.mark("pe"), psC)

                def feed_dn_post(m):
                    mm_m, psC = dn_mm[m]
                    eb = m % 2
                    evw = [mm_m] + ([dn_ev[m - 2][2]] if m >= 2 else [])
                    P.emit("dve", lambda e, p_=psC, b_=eb: e.tensor_copy(
                        ev_a[:, b_ * TC:(b_ + 1) * TC], p_[:]), waits=evw)
                    evm = P.mark("dve")
                    gpw = P.emit("sy", lambda e, m_=m, b_=eb, t_=t0: _br2(
                        e,
                        lambda e2: e2.dma_start(
                            out=bass.AP(sh_ff, 0 * Hd * S + m_ * 128 * S + t_,
                                        [[S, 128], [1, TC]]),
                            in_=ev_a[:, b_ * TC:(b_ + 1) * TC]),
                        lambda e2: e2.dma_start(
                            out=bass.AP(sh_ff, 1 * Hd * S + m_ * 128 * S + t_,
                                        [[S, 128], [1, TC]]),
                            in_=ev_a[:, b_ * TC:(b_ + 1) * TC]),
                        m_xs["st"]), waits=[evm], inc=16, sem="st")
                    dn_ev.append((m, evm, gpw))
                    if "ffp" in taps and c == 0:
                        P.emit("gp", lambda e, m_=m, b_=eb: e.dma_start(
                            out=tap_t["ffp"].ap()[:, m_ * TC:(m_ + 1) * TC],
                            in_=ev_a[:, b_ * TC:(b_ + 1) * TC]), waits=[evm], inc=16)
                        P.barrier()

                yg_marks = []
                yst_marks = []
                zsp_marks = []
                dsp_marks = []
                sc_dbu, sc_cm, sc_scan = [], [], []
                for k in range(ND):
                    if 1 <= k <= 11:
                        feed_gu_post(2 * (k - 1))
                        feed_gu_post(2 * (k - 1) + 1)
                    db = k % 2
                    w_dt = [dti_m]
                    if k >= 1:
                        w_dt.append(zsp_marks[k - 1])
                    P.emit("pe", lambda e, k_=k: e.matmul(
                        ps0[:, 0:TC], c_dtw[:, k_ * 128:(k_ + 1) * 128], dtin[:],
                        start=True, stop=True), waits=w_dt)
                    dtp_m = P.mark("pe")
                    zw = [dtp_m]
                    if k >= 1:
                        zw.append(dsp_marks[k - 1])
                    P.emit("act", lambda e, k_=k: e.activation(
                        out=z_sp[:, 0:TC], in_=ps0[:, 0:TC], func=AF.Identity,
                        bias=c_dtb[:, k_:k_ + 1]), waits=zw)
                    zsp_marks.append(P.mark("act"))
                    aw = [dsp_marks[k - 2]] if k >= 2 else []
                    P.emit("act", lambda e, b_=db: e.activation(
                        out=dtsp[:, b_ * TC:(b_ + 1) * TC], in_=z_sp[:, 0:TC],
                        func=AF.Abs), waits=aw)
                    P.emit("act", lambda e, b_=db: e.activation(
                        out=dtsp[:, b_ * TC:(b_ + 1) * TC],
                        in_=dtsp[:, b_ * TC:(b_ + 1) * TC], func=AF.Exp, scale=-1.0))
                    P.emit("act", lambda e, b_=db: e.activation(
                        out=dtsp[:, b_ * TC:(b_ + 1) * TC],
                        in_=dtsp[:, b_ * TC:(b_ + 1) * TC], func=AF.Ln, bias=1.0))
                    spl_m = P.mark("act")
                    P.emit("dve", lambda e, b_=db: e.scalar_tensor_tensor(
                        dtsp[:, b_ * TC:(b_ + 1) * TC], z_sp[:, 0:TC], 0.0,
                        dtsp[:, b_ * TC:(b_ + 1) * TC], ALU.max, ALU.add),
                        waits=[spl_m])
                    P.emit("dve", lambda e, k_=k, b_=db: e.tensor_tensor(
                        dtx[:, b_ * TC:(b_ + 1) * TC], dtsp[:, b_ * TC:(b_ + 1) * TC],
                        xs_t[:, k_ * TC:(k_ + 1) * TC], ALU.mult))
                    dsp_m = P.mark("dve")
                    dsp_marks.append(dsp_m)
                    gbuf = k % 2
                    gsl = slice(gbuf * NKT * 128, (gbuf + 1) * NKT * 128)
                    gw = [xp_done] if k < 2 else [yg_marks[k - 2][3]]
                    lwg = P.emit("sy", lambda e, k_=k, ws=gsl: e.dma_start(
                        out=w1b[:, ws],
                        in_=bass.AP(w1g, k_ * NKT * 128,
                                    [[NKT * NKT * 128, 128], [1, NKT * 128]])),
                        waits=gw, inc=16, sem="wg0" if k % 2 == 0 else "wg1")
                    psG = ps1 if k % 2 == 0 else ps6
                    w_g = [lwg, dti_m]
                    if k >= 2:
                        w_g.append(yg_marks[k - 2][2])
                    for kt in range(NKT):
                        P.emit("pe", lambda e, k2=kt, ws=gsl, p_=psG: e.matmul(
                            p_[:], w1b[:, ws.start + k2 * 128:ws.start + (k2 + 1) * 128],
                            xnT[:, k2 * TC:(k2 + 1) * TC],
                            start=(k2 == 0), stop=(k2 == NKT - 1)),
                            waits=(w_g if kt == 0 else []))
                    gmm_m = P.mark("pe")
                    if k <= 10:
                        feed_gu_mm(2 * k)
                        feed_gu_mm(2 * k + 1)
                        dn_ms = []
                    else:
                        dn_ms = list(range(3 * (k - 11), 3 * (k - 11) + 3))
                        if k == ND - 1:
                            dn_ms.append(15)
                        for m_ in dn_ms:
                            feed_dn_mm(m_)
                    for n in range(NN):
                        h = n % 2
                        brh = brep if h == 0 else war[:, 6656:6656 + TC]
                        crh = crep if h == 0 else war[:, 7168:7168 + TC]
                        dah = dAt if h == 0 else war[:, 7680:7680 + TC]
                        wb_ = [bc_hbm]
                        if len(sc_dbu) >= 2:
                            wb_.append(sc_dbu[-2])
                        lb = P.emit("gp", lambda e, n_=n, t_=t0, d=brh: e.dma_start(
                            out=d[:, 0:TC] if not isinstance(d, bass.AP) else d,
                            in_=bass.AP(bscr, n_ * S + t_, [[0, 128], [1, TC]])),
                            waits=wb_, inc=16, sem="b0" if h == 0 else "b1")
                        wc_ = [bc_hbm]
                        if len(sc_cm) >= 2:
                            wc_.append(sc_cm[-2])
                        lcp = P.emit("gp", lambda e, n_=n, t_=t0, d=crh: e.dma_start(
                            out=d[:, 0:TC] if not isinstance(d, bass.AP) else d,
                            in_=bass.AP(cscr, n_ * S + t_, [[0, 128], [1, TC]])),
                            waits=wc_, inc=16, sem="c0" if h == 0 else "c1")
                        daw = [dsp_m]
                        if len(sc_scan) >= 2:
                            daw.append(sc_scan[-2])
                        P.emit("act", lambda e, k_=k, n_=n, d_=db, o=dah: e.activation(
                            out=o[:, 0:TC] if not isinstance(o, bass.AP) else o,
                            in_=dtsp[:, d_ * TC:(d_ + 1) * TC], func=AF.Exp,
                            scale=c_A[:, k_ * NN + n_:k_ * NN + n_ + 1]),
                            waits=daw)
                        dA_m = P.mark("act")
                        P.emit("dve", lambda e, d_=db, b=brh: e.tensor_tensor(
                            dBu[:], dtx[:, d_ * TC:(d_ + 1) * TC],
                            b[:, 0:TC] if not isinstance(b, bass.AP) else b,
                            ALU.mult),
                            waits=[lb, dA_m])
                        sc_dbu.append(P.mark("dve"))
                        P.emit("dve", lambda e, k_=k, n_=n, o=dah: e.tensor_tensor_scan(
                            stt_[:], o[:, 0:TC] if not isinstance(o, bass.AP) else o,
                            dBu[:],
                            carry[:, k_ * NN + n_:k_ * NN + n_ + 1], ALU.mult, ALU.add))
                        scn_m = P.mark("dve")
                        sc_scan.append(scn_m)
                        P.emit("dve", lambda e, k_=k, n_=n: e.tensor_copy(
                            carry[:, k_ * NN + n_:k_ * NN + n_ + 1],
                            stt_[:, TC - 1:TC]), waits=[scn_m])
                        if n == 0:
                            P.emit("dve", lambda e, cc=crh: e.tensor_tensor(
                                ydt[:], stt_[:],
                                cc[:, 0:TC] if not isinstance(cc, bass.AP) else cc,
                                ALU.mult), waits=[lcp])
                            sc_cm.append(P.mark("dve"))
                        else:
                            P.emit("dve", lambda e, cc=crh: e.tensor_tensor(
                                dBu[:], stt_[:],
                                cc[:, 0:TC] if not isinstance(cc, bass.AP) else cc,
                                ALU.mult),
                                waits=[lcp])
                            sc_cm.append(P.mark("dve"))
                            P.emit("dve", lambda e: e.tensor_tensor(
                                ydt[:], ydt[:], dBu[:], ALU.add))
                    P.emit("dve", lambda e, k_=k: e.scalar_tensor_tensor(
                        ydt[:], xs_t[:, k_ * TC:(k_ + 1) * TC], c_dv[:, k_:k_ + 1],
                        ydt[:], ALU.mult, ALU.add))
                    gsw = [gmm_m] + ([yst_marks[k - 1]] if k >= 1 else [])
                    P.emit("act", lambda e, p_=psG: e.activation(
                        out=sg_t[:, 0:TC], in_=p_[:], func=AF.Silu), waits=gsw)
                    sg_m = P.mark("act")
                    P.emit("dve", lambda e: e.tensor_tensor(
                        sg_t[:, 0:TC], ydt[:], sg_t[:, 0:TC], ALU.mult),
                        waits=[sg_m])
                    ygm = P.mark("dve")
                    yst_marks.append(P.emit("sy", lambda e, k_=k: e.dma_start(
                        out=ygs.ap()[:, k_ * TC:(k_ + 1) * TC],
                        in_=sg_t[:, 0:TC]), waits=[ygm], inc=16, sem="st"))
                    if k > 10:
                        for m_ in dn_ms:
                            feed_dn_post(m_)
                    yg_marks.append((k, ygm, P.mark("act"), P.mark("pe")))
                h_done = h_marks[-1][1]
                ffst_done = P.mark("st")
                if "hffn" in taps and c == 0:
                    P.emit("gp", lambda e: e.dma_start(out=tap_t["hffn"].ap(), in_=hff[:]),
                           waits=[h_done], inc=16)
                    P.barrier()
                pe_d = P.mark("pe")
                for k in range(ND):
                    P.emit("sy", lambda e, k_=k: e.dma_start(
                        out=hff[:, k_ * TC:(k_ + 1) * TC],
                        in_=ygs.ap()[:, k_ * TC:(k_ + 1) * TC]),
                        waits=([pe_d, yst_marks[-1]] if k == 0 else []), inc=16,
                        sem="xl")
                yg_done = P.mark("xl")
                if "yg" in taps and c == 0:
                    P.emit("gp", lambda e: e.dma_start(
                        out=tap_t["yg"].ap(), in_=hff[:, 0:ND * TC]),
                        waits=[yg_done], inc=16)
                    P.barrier()

                at_ev = []
                ow_marks = {}
                omm = {}

                def ld_wob(m):
                    buf = m % 2
                    wsl = slice(buf * NKT * 128, (buf + 1) * NKT * 128)
                    lww = [dn_ev[-1][1]] if m < 2 else [omm[m - 2]]
                    ow_marks[m] = (P.emit("sy", lambda e, m_=m, ws=wsl: e.dma_start(
                        out=wob[:, ws],
                        in_=bass.AP(wo, m_ * ND * 128,
                                    [[ND * NKT * 128, 128], [1, ND * 128]])),
                        waits=lww, inc=16, sem="ow0" if m % 2 == 0 else "ow1"), wsl)

                ld_wob(0)
                ld_wob(1)
                for m in range(NKT):
                    lw, wsl = ow_marks[m]
                    psO = ps2 if m % 2 == 0 else ps3
                    w_mm = [lw, yg_done, h_done]
                    if m < 2:
                        w_mm.append(dn_ev[-1][1])
                    if m >= 2:
                        w_mm.append(at_ev[m - 2][1])
                    for kt in range(ND):
                        P.emit("pe", lambda e, k_=kt, ws=wsl, p_=psO: e.matmul(
                            p_[:], wob[:, ws.start + k_ * 128:ws.start + (k_ + 1) * 128],
                            hff[:, k_ * TC:(k_ + 1) * TC],
                            start=(k_ == 0), stop=(k_ == ND - 1)),
                            waits=(w_mm if kt == 0 else []))
                    mm_m = P.mark("pe")
                    omm[m] = mm_m
                    if m + 2 < NKT:
                        ld_wob(m + 2)
                    eb = m % 2
                    evw = [mm_m]
                    evw.append(at_ev[m - 2][2] if m >= 2 else ffst_done)
                    P.emit("dve", lambda e, p_=psO, b_=eb: e.tensor_copy(
                        ev_a[:, b_ * TC:(b_ + 1) * TC], p_[:]), waits=evw)
                    evm = P.mark("dve")
                    gpo = P.emit("gp", lambda e, m_=m, b_=eb, t_=t0: _br2(
                        e,
                        lambda e2: e2.dma_start(
                            out=bass.AP(sh_at, 0 * Hd * S + m_ * 128 * S + t_,
                                        [[S, 128], [1, TC]]),
                            in_=ev_a[:, b_ * TC:(b_ + 1) * TC]),
                        lambda e2: e2.dma_start(
                            out=bass.AP(sh_at, 1 * Hd * S + m_ * 128 * S + t_,
                                        [[S, 128], [1, TC]]),
                            in_=ev_a[:, b_ * TC:(b_ + 1) * TC]),
                        m_gp), waits=[evm], inc=16)
                    at_ev.append((m, evm, gpo))
                    if "atp" in taps and c == 0:
                        P.emit("gp", lambda e, m_=m, b_=eb: e.dma_start(
                            out=tap_t["atp"].ap()[:, m_ * TC:(m_ + 1) * TC],
                            in_=ev_a[:, b_ * TC:(b_ + 1) * TC]), waits=[evm], inc=16)
                        P.barrier()

                st_done = P.mark("st")
                st_done2 = P.mark("gp")
                P.emit("sy", lambda e, c_=c: _br2(
                    e,
                    lambda e2: e2.dma_start(out=bass.AP(sh_fl, 0, [[16, 1], [1, 1]]),
                                            in_=c_fv[0:1, c_:c_ + 1]),
                    lambda e2: e2.dma_start(out=bass.AP(sh_fl, 16, [[16, 1], [1, 1]]),
                                            in_=c_fv[0:1, c_:c_ + 1]),
                    m_sy), waits=[st_done, st_done2], inc=16)

            P.barrier()
            for t_, sh_ in (("shat", sh_at), ("shff", sh_ff)):
                if t_ in taps:
                    for m in range(NKT):
                        P.emit("gp", lambda e, tt=t_, ss=sh_, m_=m: _br2(
                            e,
                            lambda e2: e2.dma_start(
                                out=tap_t[tt].ap()[m_ * 128:(m_ + 1) * 128, :],
                                in_=bass.AP(ss, m_ * 128 * S, [[S, 128], [1, S]])),
                            lambda e2: e2.dma_start(
                                out=tap_t[tt].ap()[m_ * 128:(m_ + 1) * 128, :],
                                in_=bass.AP(ss, Hd * S + m_ * 128 * S, [[S, 128], [1, S]])),
                            m_gp), inc=16)
                    P.barrier()

            def poll(e):
                slot = _slotreg(e)
                expv = e.alloc_register("expr")
                e.reg_load(expv, c_fv[0:1, NCH - 1:NCH])
                fv = e.alloc_register("fvr")
                pc = e.alloc_register("pcr")
                cont = e.alloc_register("contr")
                e.reg_mov(fv, 0)
                e.reg_mov(pc, 0)
                e.reg_mov(cont, 1)
                with e.While(cont):
                    e.reg_add(pc, pc, 16)
                    with e.If_eq(slot, 0):
                        e.dma_start(out=fl_t[:1, :1],
                                    in_=bass.AP(sh_fl, 16, [[16, 1], [1, 1]])
                                    ).then_inc(m_poll, 16)
                    with e.Else():
                        e.dma_start(out=fl_t[:1, :1],
                                    in_=bass.AP(sh_fl, 0, [[16, 1], [1, 1]])
                                    ).then_inc(m_poll, 16)
                    e.wait_ge(m_poll, pc)
                    e.reg_load(fv, fl_t[:1, :1])
                    e.reg_alu(cont, fv, expv, ALU.not_equal)
                return None

            P.emit("gp", poll, inc=0)
            poll_m = P.mark("gp")

            out_prev = [None]
            f_am, f_ap = dAt[:, 0:TC], sg_t[:, 0:TC]
            f_fm, f_fp = dtsp[:, 0:TC], dtsp[:, TC:2 * TC]
            f_sq = stt_[:]
            f_o = ydt[:]
            f_rr = brep[:]
            f_x = xnT
            f_ss = dtx[0:1, 0:TC]
            f_rs = dtx[0:1, TC:2 * TC]

            for lc in range(2):
                P.barrier()
                xl_marks = []
                tp_ev = []
                for sub in range(NSUB):
                    def xload(e, s=sub, l=lc):
                        o0 = (l * TC + s * 128) * Hd
                        o1 = ((2 + l) * TC + s * 128) * Hd
                        return _br2(
                            e,
                            lambda e2: e2.dma_start(
                                out=xn1[:], in_=bass.AP(x_in, o0, [[Hd, 128], [1, Hd]])),
                            lambda e2: e2.dma_start(
                                out=xn1[:], in_=bass.AP(x_in, o1, [[Hd, 128], [1, Hd]])),
                            m_xs["xl"])
                    wts = [poll_m] + ([tp_ev[-1]] if tp_ev else [])
                    xl = P.emit("gp", xload, waits=wts, inc=16, sem="xl")
                    for kt in range(NKT):
                        pst = ps0 if (sub * NKT + kt) % 2 == 0 else ps1
                        src = bass.AP(xn1, kt * 128, [[Hd, 128], [1, 128]])
                        w = [xl]
                        if len(tp_ev) >= 2:
                            w.append(tp_ev[-2])
                        P.emit("pe", lambda e, p_=pst, s_=src: e.transpose(
                            p_[:, 0:128], s_, c_ident[:]), waits=w)
                        tpm = P.mark("pe")
                        P.emit("act", lambda e, p_=pst, k_=kt, s_=sub: e.activation(
                            out=f_x[:, k_ * TC + s_ * 128:k_ * TC + s_ * 128 + 128],
                            in_=p_[:, 0:128], func=AF.Identity), waits=[tpm])
                        tp_ev.append(P.mark("act"))
                fx_done = tp_ev[-1]
                if "xf" in taps and lc == 0:
                    P.emit("gp", lambda e: e.dma_start(out=tap_t["xf"].ap(), in_=f_x[:]),
                           waits=[fx_done], inc=16)
                    P.barrier()

                ffsum_writes = []
                sq_last = None
                for m in range(NKT):
                    def rd_fm(e, m_=m, l=lc):
                        o0 = 0 * Hd * S + m_ * 128 * S + l * TC
                        o1 = 1 * Hd * S + m_ * 128 * S + (2 + l) * TC
                        return _br2(
                            e,
                            lambda e2: e2.dma_start(
                                out=f_fm, in_=bass.AP(sh_ff, o0, [[S, 128], [1, TC]])),
                            lambda e2: e2.dma_start(
                                out=f_fm, in_=bass.AP(sh_ff, o1, [[S, 128], [1, TC]])),
                            m_gp)

                    def rd_fp(e, m_=m, l=lc):
                        o0 = 1 * Hd * S + m_ * 128 * S + l * TC
                        o1 = 0 * Hd * S + m_ * 128 * S + (2 + l) * TC
                        return _br2(
                            e,
                            lambda e2: e2.dma_start(
                                out=f_fp, in_=bass.AP(sh_ff, o0, [[S, 128], [1, TC]])),
                            lambda e2: e2.dma_start(
                                out=f_fp, in_=bass.AP(sh_ff, o1, [[S, 128], [1, TC]])),
                            m_gp)

                    prev = P.mark("dve")
                    preva = P.mark("act")
                    w_rd = [prev, preva]
                    if ffsum_writes:
                        w_rd.append(ffsum_writes[-1])
                    r1 = P.emit("gp", rd_fm, waits=w_rd, inc=16)
                    r2 = P.emit("gp", rd_fp, inc=16)
                    P.emit("dve", lambda e: e.tensor_tensor(f_fm, f_fm, f_fp, ALU.add),
                           waits=[r1, r2])
                    fsum_m = P.mark("dve")
                    ffw = P.emit("gp", lambda e, m_=m, l=lc: e.dma_start(
                        out=ffsum.ap()[m_ * 128:(m_ + 1) * 128, l * TC:(l + 1) * TC],
                        in_=f_fm), waits=[fsum_m], inc=16, sem="st")
                    ffsum_writes.append(ffw)
                    sq_w = [fsum_m]
                    if sq_last is not None:
                        sq_w.append(sq_last)
                    P.emit("act", lambda e: e.activation(out=f_sq, in_=f_fm,
                                                         func=AF.Square),
                           waits=sq_w)
                    sqm = P.mark("act")
                    P.emit("pe", lambda e, m_=m: e.matmul(
                        ps2[0:1, 0:TC], c_ones[:], f_sq,
                        start=(m_ == 0), stop=(m_ == NKT - 1)), waits=[sqm])
                    sq_last = P.mark("pe")
                P.emit("act", lambda e: e.activation(
                    out=f_rs, in_=ps2[0:1, 0:TC], func=AF.Sqrt,
                    scale=1.0 / Hd, bias=c_eps[0:1, 1:2]), waits=[sq_last])
                rs_m = P.mark("act")
                P.emit("dve", lambda e: e.reciprocal(f_ss, f_rs), waits=[rs_m])
                rr_w = P.mark("dve")
                P.emit("gp", lambda e: e.dma_start(out=bscr.ap()[0:1, 0:TC], in_=f_ss),
                       waits=[rr_w], inc=16)
                bb_m = P.mark("gp")
                P.emit("sy", lambda e: e.dma_start(
                    out=f_rr, in_=bass.AP(bscr, 0, [[0, 128], [1, TC]])),
                    waits=[bb_m], inc=16)
                rr_m = P.mark("sy")

                for m in range(NKT):
                    def rd_am(e, m_=m, l=lc):
                        o0 = 0 * Hd * S + m_ * 128 * S + l * TC
                        o1 = 1 * Hd * S + m_ * 128 * S + (2 + l) * TC
                        return _br2(
                            e,
                            lambda e2: e2.dma_start(
                                out=f_am, in_=bass.AP(sh_at, o0, [[S, 128], [1, TC]])),
                            lambda e2: e2.dma_start(
                                out=f_am, in_=bass.AP(sh_at, o1, [[S, 128], [1, TC]])),
                            m_gp)

                    def rd_ap(e, m_=m, l=lc):
                        o0 = 1 * Hd * S + m_ * 128 * S + l * TC
                        o1 = 0 * Hd * S + m_ * 128 * S + (2 + l) * TC
                        return _br2(
                            e,
                            lambda e2: e2.dma_start(
                                out=f_ap, in_=bass.AP(sh_at, o0, [[S, 128], [1, TC]])),
                            lambda e2: e2.dma_start(
                                out=f_ap, in_=bass.AP(sh_at, o1, [[S, 128], [1, TC]])),
                            m_gp)

                    def rd_fs(e, m_=m, l=lc):
                        return e.dma_start(
                            out=f_fm,
                            in_=ffsum.ap()[m_ * 128:(m_ + 1) * 128, l * TC:(l + 1) * TC])

                    prev = P.mark("dve")
                    r1 = P.emit("gp", rd_am, waits=[prev], inc=16)
                    r2 = P.emit("gp", rd_ap, inc=16)
                    r3 = P.emit("gp", rd_fs, waits=[ffsum_writes[-1]], inc=16)
                    P.emit("dve", lambda e: e.tensor_tensor(f_am, f_am, f_ap, ALU.add),
                           waits=[r1, r2, rr_m, fx_done])
                    P.emit("dve", lambda e, m_=m: e.tensor_tensor(
                        f_am, f_am, f_x[:, m_ * TC:(m_ + 1) * TC], ALU.add))
                    P.emit("dve", lambda e: e.tensor_tensor(f_fp, f_fm, f_rr, ALU.mult),
                           waits=[r3])
                    P.emit("dve", lambda e: e.tensor_tensor(f_o, f_am, f_fp, ALU.add),
                           waits=([out_prev[0]] if out_prev[0] else []))
                    om = P.mark("dve")
                    out_prev[0] = P.emit("gp", lambda e, m_=m, l=lc: e.dma_start(
                        out=out.ap()[m_ * 128:(m_ + 1) * 128, l * TC:(l + 1) * TC],
                        in_=f_o), waits=[om], inc=16, sem="dq")

            last_wait = {}

            def emit_stream(eng_name, engine):
                for waits, fn, inc, sem in P.q[eng_name]:
                    for sem_name, val in waits:
                        key = (eng_name, sem_name)
                        if val > 0 and last_wait.get(key, -1) < val:
                            engine.wait_ge(SEM[sem_name], val)
                            last_wait[key] = val
                    inst = fn(engine)
                    if inc:
                        if inst == "incd":
                            pass
                        elif inst is not None:
                            inst.then_inc(SEM[sem], inc)
                        else:
                            engine.sem_inc(SEM[sem], inc)

            @block.tensor
            def _(t):
                emit_stream("pe", t)

            @block.vector
            def _(v):
                emit_stream("dve", v)

            @block.scalar
            def _(s):
                emit_stream("act", s)

            @block.sync
            def _(s):
                emit_stream("sy", s)

            @block.gpsimd
            def _(g):
                emit_stream("gp", g)
                g.wait_ge(m_gp, P.cnt["gp"])
                for s_ in XSEMS:
                    if P.cnt[s_] > 0:
                        g.wait_ge(m_xs[s_], P.cnt[s_])

    return nc


_GPR = {}


def _slotreg(e):
    key = id(e)
    if key not in _GPR:
        pid = e.partition_id()
        slot = e.alloc_register("slotr")
        e.reg_alu(slot, pid, 1, ALU.bitwise_and)
        _GPR[key] = slot
    return _GPR[key]


def _br2(e, fn0, fn1, incsem=None):
    slot = _slotreg(e)
    with e.If_eq(slot, 0):
        i0 = fn0(e)
        if incsem is not None and i0 is not None:
            i0.then_inc(incsem, 16)
    with e.Else():
        i1 = fn1(e)
        if incsem is not None and i1 is not None:
            i1.then_inc(incsem, 16)
    return "incd"


def kernel(**inp):
    global _LAST
    x_full = np.asarray(inp["L0_attn_hidden_states"])
    W1 = np.asarray(inp["L0_attn_in_proj_w"])
    cw = np.asarray(inp["L0_attn_conv_w"])
    cb = np.asarray(inp["L0_attn_conv_b"])
    xpw = np.asarray(inp["L0_attn_x_proj_weight"])
    dtw = np.asarray(inp["L0_attn_dt_proj_weight"])
    dtb = np.asarray(inp["L0_attn_dt_proj_bias"])
    alog = np.asarray(inp["L0_attn_A_log"])
    Dv = np.asarray(inp["L0_attn_D"])
    Wo = np.asarray(inp["L0_attn_out_proj_w"])
    gup = np.asarray(inp["L0_ffn_gate_up_packed"])
    gus = np.asarray(inp["L0_ffn_gate_up_scales"])
    dnp = np.asarray(inp["L0_ffn_down_packed"])
    dns = np.asarray(inp["L0_ffn_down_scales"])

    hperm = np.array([256 * (kt >> 1) + 2 * p + (kt & 1)
                      for kt in range(NKT) for p in range(128)])
    nonce = int.from_bytes(os.urandom(3), "little") + 7
    fv_nonce = (nonce + np.arange(1, 9, dtype=np.int64)).astype(np.int32).reshape(1, 8)
    in_maps = []
    for c in range(8):
        b, hh = c >> 1, c & 1
        isl = slice(hh * HD, (hh + 1) * HD)
        rows = 2 * np.arange(hh * HD, (hh + 1) * HD)
        in_maps.append(dict(
            x_in=np.ascontiguousarray(x_full[b]),
            w1x=np.ascontiguousarray(
                W1[rows, :].T[hperm].reshape(NKT, 128, NKT, 128)
                .transpose(1, 2, 0, 3).reshape(128, -1)),
            w1g=np.ascontiguousarray(
                W1[rows + 1, :].T[hperm].reshape(NKT, 128, NKT, 128)
                .transpose(1, 2, 0, 3).reshape(128, -1)),
            xw=np.ascontiguousarray(xpw[hh].T),
            dtw=np.ascontiguousarray(dtw[hh].T),
            dtbias=np.ascontiguousarray(dtb[hh].reshape(ND, 128).T),
            alog=np.ascontiguousarray(
                alog[hh].reshape(ND, 128, NN).transpose(1, 0, 2).reshape(128, ND * NN)),
            dvec=np.ascontiguousarray(Dv[hh].reshape(ND, 128).T),
            convw=np.ascontiguousarray(
                cw[isl].reshape(ND, 128, 4).transpose(1, 0, 2).reshape(128, ND * 4)),
            convb=np.ascontiguousarray(cb[isl].reshape(ND, 128).T),
            wo=np.ascontiguousarray(
                Wo[:, isl].T.reshape(ND, 128, NKT, 128)
                .transpose(1, 2, 0, 3).reshape(128, -1)),
            gpk=np.ascontiguousarray(
                gup[hh * OG:(hh + 1) * OG].view(np.uint8).reshape(OG, Hd // 2).T),
            upk=np.ascontiguousarray(
                gup[2 * OG + hh * OG:2 * OG + (hh + 1) * OG].view(np.uint8)
                .reshape(OG, Hd // 2).T),
            dpk=np.ascontiguousarray(
                dnp.view(np.uint8).reshape(Hd, OG)[:, hh * (OG // 2):(hh + 1) * (OG // 2)].T),
            gsc=np.ascontiguousarray(gus[:, hh * OG:(hh + 1) * OG]),
            usc=np.ascontiguousarray(gus[:, 2 * OG + hh * OG:2 * OG + (hh + 1) * OG]),
            dsc=np.ascontiguousarray(dns[hh * 22:(hh + 1) * 22]),
            fvals=fv_nonce,
        ))

    nc = build_nc(taps=_TAPS)
    res = run_bass_kernel_spmd(nc, in_maps, core_ids=list(range(8)))
    _LAST = {c: res.results[c] for c in range(8)}

    outp = np.empty((4, S, Hd), dtype=np.float32)
    for c in range(8):
        b, hh = c >> 1, c & 1
        outp[b, hh * (S // 2):(hh + 1) * (S // 2), :] = res.results[c]["out"].T
    return outp
